# version 06
# baseline (speedup 1.0000x reference)
import numpy as np
import ml_dtypes
import concourse.bass as bass
import concourse.mybir as mybir
from concourse.bass_utils import run_bass_kernel_spmd
from contextlib import ExitStack

F32 = mybir.dt.float32
BF16 = mybir.dt.bfloat16
I32 = mybir.dt.int32
AF = mybir.ActivationFunctionType
ALU = mybir.AluOpType
AX = mybir.AxisListType

T = 8192
D = 4096
DC = 2048
NH = 8
HD = 256
TT = 512
NT = T // TT
QB = 512
NQB = 8
NFC = 32
NOC = DC // 128
EPS = 1e-5
NPOLL = 4000000

ENGS = ("sync", "tensor", "vector", "scalar", "gpsimd")


class Sched:
    def __init__(self):
        self.ops = {e: [] for e in ENGS}
        self.cnt = {}

    def inc(self, name, amt=1):
        self.cnt[name] = self.cnt.get(name, 0) + amt
        return self.cnt[name]

    def val(self, name):
        return self.cnt.get(name, 0)

    def on(self, eng, fn, waits=(), inc_sem=None, amt=1):
        w = [x for x in waits if x is not None and x[1] > 0]
        mark = None
        if inc_sem is not None:
            mark = (inc_sem, self.inc(inc_sem, amt))
        self.ops[eng].append((fn, w, (inc_sem, amt) if inc_sem else None))
        return mark

    def dma(self, eng, sem, out, in_, waits=()):
        o, i = out, in_
        return self.on(eng, lambda e, c: e.dma_start(out=o, in_=i),
                       waits, sem, 16)

    def fence(self, sem):
        return (sem, self.val(sem))


class Roll:

    def __init__(self, n=2):
        self.m = {i: None for i in range(n)}

    def wait(self, slot):
        return self.m[slot]

    def used(self, slot, mark):
        self.m[slot] = mark


def _emit(nc, sch, semh, ectx):
    with ExitStack() as bctx:
        block = bctx.enter_context(nc.Block())

        def mk(ename):
            def run(eng):
                ctx = ectx[ename]
                ctx["eng"] = eng
                for fn, waits, inc in sch.ops[ename]:
                    for (n, v) in waits:
                        eng.wait_ge(semh[n], v)
                    ins = fn(eng, ctx)
                    if inc is not None:
                        if ins is None:
                            eng.nop().then_inc(semh[inc[0]], inc[1])
                        else:
                            ins.then_inc(semh[inc[0]], inc[1])
            return run

        @block.sync
        def _(e):
            mk("sync")(e)

        @block.tensor
        def _(e):
            mk("tensor")(e)

        @block.vector
        def _(e):
            mk("vector")(e)

        @block.scalar
        def _(e):
            mk("scalar")(e)

        @block.gpsimd
        def _(e):
            mk("gpsimd")(e)


def _build(dbg=0):
    nc = bass.Bass("TRN2", target_bir_lowering=False)

    x_fm = nc.dram_tensor("x_fm", [NFC, 128, T], F32, kind="ExternalInput").ap()
    wq_t = nc.dram_tensor("wq_t", [NOC, 128, D], F32, kind="ExternalInput").ap()
    wk_t = nc.dram_tensor("wk_t", [NOC, 128, D], F32, kind="ExternalInput").ap()
    wv_t = nc.dram_tensor("wv_t", [NOC, 128, D], F32, kind="ExternalInput").ap()
    wo_t = nc.dram_tensor("wo_t", [NFC, 128, D], F32, kind="ExternalInput").ap()
    a_st = nc.dram_tensor("a_st", [NFC, 128, 68], F32, kind="ExternalInput").ap()
    b_pk = nc.dram_tensor("b_pk", [64, D], F32, kind="ExternalInput").ap()
    e_tab = nc.dram_tensor("e_tab", [4, 64], F32, kind="ExternalInput").ap()
    pi_tab = nc.dram_tensor("pi_tab", [64, 64], F32, kind="ExternalInput").ap()
    rotck = nc.dram_tensor("rotck", [64, T], F32, kind="ExternalInput").ap()
    rotsk = nc.dram_tensor("rotsk", [64, T], F32, kind="ExternalInput").ap()
    rotcq = nc.dram_tensor("rotcq", [64, T // 2], F32, kind="ExternalInput").ap()
    rotsq = nc.dram_tensor("rotsq", [64, T // 2], F32, kind="ExternalInput").ap()
    maskb = nc.dram_tensor("maskb", [8, 128, QB], F32, kind="ExternalInput").ap()
    identf = nc.dram_tensor("identf", [128, 128], F32, kind="ExternalInput").ap()
    offs = nc.dram_tensor("offs", [1, 16], I32, kind="ExternalInput").ap()

    dbg_kind = "ExternalOutput" if dbg == 1 else "Internal"
    u_my = nc.dram_tensor("u_my", [NFC, 128, NQB * QB], F32, kind=dbg_kind)
    k_fm = nc.dram_tensor("k_fm", [NH, 2, 128, T], BF16, kind=dbg_kind)
    v_tm = nc.dram_tensor("v_tm", [NH, T, HD], BF16, kind=dbg_kind)
    q_my = nc.dram_tensor("q_my", [NH, 2, 128, NQB * QB], BF16, kind=dbg_kind)
    nrm_dbg = nc.dram_tensor("nrm_dbg", [2, NH], F32, kind=dbg_kind)
    o_loc = nc.dram_tensor("o_loc", [DC, NQB * QB], BF16, kind="Internal")
    u_sel = nc.dram_tensor("u_sel", [NFC, 128, 4 * QB], F32, kind="Internal")
    sh_O = nc.dram_tensor("sh_O", [D, NQB * QB], BF16, kind="Internal",
                          addr_space="Shared")
    sh_flag = nc.dram_tensor("sh_flag", [2, 16], I32, kind="Internal",
                             addr_space="Shared")
    o_dbg = nc.dram_tensor("o_dbg", [D, NQB * QB], BF16,
                           kind="ExternalOutput" if dbg == 2 else "Internal")
    vbf_h = nc.dram_tensor("vbf_h", [NFC, 128, 4 * QB], BF16, kind="Internal")
    w_bf = nc.dram_tensor("w_bf", [3, NOC, 128, D], BF16, kind="Internal")
    v_my = nc.dram_tensor("v_my", [NFC, 128, 4 * QB], F32, kind="Internal")
    out_my = nc.dram_tensor("out_my", [NFC, 128, 4 * QB], F32,
                            kind="ExternalOutput")

    sch = Sched()
    ectx = {e: {} for e in ENGS}
    WT = {"k": wk_t, "v": wv_t, "q": wq_t}

    with ExitStack() as top:
        E = top.enter_context

        ast_sb = E(nc.sbuf_tensor("ast_sb", [128, NFC * 68], BF16))
        bpk_sb = E(nc.sbuf_tensor("bpk_sb", [64, D], BF16))
        etab_sb = E(nc.sbuf_tensor("etab_sb", [4, 64], BF16))
        pi_sb = E(nc.sbuf_tensor("pi_sb", [64, 64], F32))
        mask_sb = E(nc.sbuf_tensor("mask_sb", [128, 8 * QB], BF16))
        ident_sb = E(nc.sbuf_tensor("ident_sb", [128, 128], BF16))
        ones_bf = E(nc.sbuf_tensor("ones_bf", [128, 1], BF16))
        ones_f = E(nc.sbuf_tensor("ones_f", [1, 128], F32))
        offs_sb = E(nc.sbuf_tensor("offs_sb", [1, 16], I32))
        csA_b = E(nc.sbuf_tensor("csA_b", [1, 68], BF16))
        qmax_sb = E(nc.sbuf_tensor("qmax_sb", [1, NH], F32))
        kmax_sb = E(nc.sbuf_tensor("kmax_sb", [1, NH], F32))
        bias_sb = E(nc.sbuf_tensor("bias_sb", [128, NH], F32))
        flagw_sb = E(nc.sbuf_tensor("flagw_sb", [1, 16], I32))
        flagr_sb = E(nc.sbuf_tensor("flagr_sb", [1, 16], I32))

        semh = {}
        for s in ("ds", "dg", "da", "tE", "vE", "aE", "gE", "pp"):
            semh[s] = E(nc.semaphore(s))

        m_cons = [
            sch.dma("gpsimd", "dg", ast_sb.rearrange("p (c k) -> p c k", k=68),
                    a_st.rearrange("c p k -> p c k")),
            sch.dma("gpsimd", "dg", bpk_sb[:, :], b_pk),
            sch.dma("gpsimd", "dg", etab_sb[:, :], e_tab),
            sch.dma("gpsimd", "dg", mask_sb.rearrange("p (s c) -> p s c",
                                                      c=QB),
                    maskb.rearrange("s p c -> p s c")),
            sch.dma("gpsimd", "dg", ident_sb[:, :], identf),
            sch.dma("sync", "ds", pi_sb[:, :], pi_tab),
            sch.dma("sync", "ds", offs_sb[:, :], offs),
        ]

        def _init(e, c):
            e.memset(ones_bf[:, :], 1.0)
            e.memset(ones_f[:, :], 1.0)
            e.memset(qmax_sb[:, :], 0.0)
            return e.memset(kmax_sb[:, :], 0.0)
        m_init = sch.on("vector", _init, m_cons, "vE")

        fe = ExitStack()
        Ef = fe.enter_context
        csW_b = Ef(nc.sbuf_tensor("csW_b", [1, 3 * NOC * 128], BF16))
        xb = Ef(nc.sbuf_tensor("xb", [128, NFC * TT], BF16))
        xf = [Ef(nc.sbuf_tensor(f"xf{i}", [128, TT], F32)) for i in range(2)]
        sqb = [Ef(nc.sbuf_tensor(f"sqb{i}", [128, TT], BF16)) for i in range(2)]
        wsb = [Ef(nc.sbuf_tensor(f"wsb{i}", [128, D], BF16)) for i in range(2)]
        uf = [Ef(nc.sbuf_tensor(f"uf{i}", [128, TT], F32)) for i in range(2)]
        mid_f = Ef(nc.sbuf_tensor("mid_f", [68, TT], F32))
        e4 = Ef(nc.sbuf_tensor("e4", [4, TT], F32))
        g4_b = Ef(nc.sbuf_tensor("g4_b", [4, TT], BF16))
        gmid_b = Ef(nc.sbuf_tensor("gmid_b", [64, TT], BF16))
        r1bc = Ef(nc.sbuf_tensor("r1bc", [68, TT], F32))
        r2bc = Ef(nc.sbuf_tensor("r2bc", [128, TT], F32))
        RK = ("mean", "var", "rstd", "nmean", "tmp")
        srow = {k: Ef(nc.sbuf_tensor(f"s_{k}", [1, TT], F32)) for k in RK}
        nmean_b = Ef(nc.sbuf_tensor("nmean_b", [1, TT], BF16))
        GK = ("r0", "r1", "r2", "r3", "w0", "w1", "M", "T2", "den", "rec")
        grow = {k: Ef(nc.sbuf_tensor(f"g_{k}", [1, TT], F32)) for k in GK}
        erow = [Ef(nc.sbuf_tensor(f"e_{i}", [1, TT], F32)) for i in range(4)]
        kvq_b = [Ef(nc.sbuf_tensor(f"kvq{i}", [128, TT], BF16))
                 for i in range(2)]
        ksc = Ef(nc.sbuf_tensor("ksc", [64, TT], F32))
        kb1 = Ef(nc.sbuf_tensor("kb1", [64, TT], F32))
        rotc_sb = Ef(nc.sbuf_tensor("rotc_sb", [64, TT], F32))
        rots_sb = Ef(nc.sbuf_tensor("rots_sb", [64, TT], F32))
        rotcq_sb = Ef(nc.sbuf_tensor("rotcq_sb", [64, TT], F32))
        rotsq_sb = Ef(nc.sbuf_tensor("rotsq_sb", [64, TT], F32))
        vrow = Ef(nc.sbuf_tensor("vrow", [128, 4 * DC], BF16))
        nrm1 = Ef(nc.sbuf_tensor("nrm1", [1, 1], F32))

        ps_stA = Ef(nc.psum_tensor("ps_stA", [1, TT], F32))
        ps_stB = Ef(nc.psum_tensor("ps_stB", [1, TT], F32))
        ps_mid = Ef(nc.psum_tensor("ps_mid", [68, TT], F32))
        ps_bc = Ef(nc.psum_tensor("ps_bc", [128, TT], F32))
        ps_o = [Ef(nc.psum_tensor(f"ps_o{i}", [128, TT], F32))
                for i in range(2)]
        ps_aux = Ef(nc.psum_tensor("ps_aux", [64, TT], F32))
        ps_vtb = Ef(nc.psum_tensor("ps_vtb", [128, 4 * 128], BF16))

        r_xf = Roll()
        r_sqb = Roll()
        r_wsb = Roll()
        r_uf = Roll()
        r_kvq = Roll()
        r_pso = Roll()
        r_stat = Roll(1)
        r_mid = Roll(1)
        r_bc = Roll(1)
        r_aux = Roll(1)
        r_nrmrow = Roll(1)
        r_vtb = Roll(1)
        r_vrow = Roll(1)
        r_gate = {}

        def gw(name):
            return r_gate.get(name)

        def gs(name, mark):
            r_gate[name] = mark

        def mm_csa(e, c):
            last = None
            for fc in range(NFC):
                last = e.matmul(ps_aux[0:1, 0:68], ones_bf[:, 0:1],
                                ast_sb[:, fc * 68:(fc + 1) * 68],
                                start=(fc == 0), stop=(fc == NFC - 1))
            return last
        m = sch.on("tensor", mm_csa, m_cons + [m_init], "tE")
        m_csA = sch.on("vector",
                       lambda e, c: e.tensor_copy(csA_b[:, :],
                                                  ps_aux[0:1, 0:68]),
                       [m], "vE")
        r_nrmrow.used(0, m_csA)
        r_wst = Roll()
        cache_mark = {}
        cs_prev = [None]

        def ln_stats(wait_mark):
            def f2(e, c):
                e.tensor_scalar_mul(srow["mean"][:, :], ps_stA[:, :], 1.0 / D)
                e.tensor_scalar_mul(srow["tmp"][:, :], ps_stB[:, :], 1.0 / D)
                e.tensor_mul(srow["var"][:, :], srow["mean"][:, :],
                             srow["mean"][:, :])
                e.tensor_sub(srow["var"][:, :], srow["tmp"][:, :],
                             srow["var"][:, :])
                e.tensor_scalar_add(srow["var"][:, :], srow["var"][:, :], EPS)
                e.tensor_scalar_mul(srow["nmean"][:, :], srow["mean"][:, :],
                                    -1.0)
                return e.tensor_copy(nmean_b[:, :], srow["nmean"][:, :])
            mv = sch.on("vector", f2, [wait_mark], "vE")

            def f2r(e, c):
                e.activation(srow["tmp"][:, :], srow["var"][:, :], AF.Ln)
                return e.activation(srow["rstd"][:, :], srow["tmp"][:, :],
                                    AF.Exp, scale=-0.5)
            ma = sch.on("scalar", f2r, [mv], "aE")
            return mv, ma

        for ti in range(NT):
            even = (ti % 2 == 0)
            tc = slice(ti * TT, (ti + 1) * TT)

            for fc in range(NFC):
                buf = fc % 2
                w = [r_xf.wait(buf)]
                if fc == 0:
                    w += [r_stat.wait(0), r_mid.wait(0), gw("xbuse")]
                mld = sch.dma("sync", "ds", xf[buf][:, :], x_fm[fc, :, tc],
                              waits=w)
                m_cast = sch.on(
                    "vector",
                    lambda e, c, fc=fc, buf=buf: e.tensor_copy(
                        xb[:, fc * TT:(fc + 1) * TT], xf[buf][:, :]),
                    [mld], "vE")
                m_sq = sch.on(
                    "scalar",
                    lambda e, c, buf=buf: e.activation(
                        sqb[buf][:, :], xf[buf][:, :], AF.Square),
                    [mld, r_sqb.wait(buf)], "aE")

                def mm_f1(e, c, fc=fc, buf=buf):
                    xs = xb[:, fc * TT:(fc + 1) * TT]
                    e.matmul(ps_stA[:, :], ones_bf[:, 0:1], xs,
                             start=(fc == 0), stop=(fc == NFC - 1))
                    e.matmul(ps_mid[:, :], ast_sb[:, fc * 68:(fc + 1) * 68],
                             xs, start=(fc == 0), stop=False)
                    return e.matmul(ps_stB[:, :], ones_bf[:, 0:1],
                                    sqb[buf][:, :], start=(fc == 0),
                                    stop=(fc == NFC - 1))
                wmm = [m_cast, m_sq]
                m_f1 = sch.on("tensor", mm_f1, wmm, "tE")
                r_xf.used(buf, m_f1)
                r_sqb.used(buf, m_f1)
                gs(("cast", fc), m_cast)

            m_ln1v, m_r1 = ln_stats(m_f1)
            r_stat.used(0, m_ln1v)

            def f3mm(e, c):
                e.matmul(ps_mid[:, :], csA_b[:, :], nmean_b[:, :],
                         start=False, stop=True)
                return e.matmul(ps_bc[0:68, :], ones_f[:, 0:68],
                                srow["rstd"][:, :], start=True, stop=True)
            m_f3 = sch.on("tensor", f3mm, [m_ln1v, m_r1, r_bc.wait(0)], "tE")

            def f3ev(e, c):
                e.tensor_copy(r1bc[:, :], ps_bc[0:68, :])
                return e.tensor_mul(mid_f[:, :], ps_mid[:, :], r1bc[:, :])
            m_midev = sch.on("vector", f3ev, [m_f3, gw("midfree")], "vE")
            r_mid.used(0, m_midev)
            r_bc.used(0, m_midev)

            w4 = [m_midev, gw("growfree")]
            for i2 in range(4):
                m_gr = sch.dma("gpsimd", "dg", grow[f"r{i2}"][:, :],
                               mid_f[64 + i2:65 + i2, :],
                               waits=w4 if i2 == 0 else ())
            m_e4 = sch.on("scalar",
                          lambda e, c: e.activation(e4[:, :],
                                                    mid_f[64:68, :], AF.Exp),
                          [m_midev, gw("e4free")], "aE")
            for i2 in range(4):
                m_ge = sch.dma("gpsimd", "dg", erow[i2][:, :],
                               e4[i2:i2 + 1, :],
                               waits=[m_e4, gw("erowfree")]
                               if i2 == 0 else ())

            def f4gate(e, c):
                g = grow
                e.tensor_max(g["w0"][:, :], g["r0"][:, :], g["r1"][:, :])
                e.tensor_tensor(g["M"][:, :], g["r0"][:, :], g["r1"][:, :],
                                ALU.min)
                e.tensor_max(g["w1"][:, :], g["r2"][:, :], g["r3"][:, :])
                e.tensor_tensor(g["T2"][:, :], g["r2"][:, :], g["r3"][:, :],
                                ALU.min)
                e.tensor_max(g["den"][:, :], g["M"][:, :], g["T2"][:, :])
                e.tensor_tensor(g["T2"][:, :], g["w0"][:, :], g["w1"][:, :],
                                ALU.min)
                e.tensor_max(g["T2"][:, :], g["T2"][:, :], g["den"][:, :])
                for i2 in range(4):
                    e.tensor_tensor(g[f"r{i2}"][:, :], g[f"r{i2}"][:, :],
                                    g["T2"][:, :], ALU.is_ge)
                for i2 in range(4):
                    e.tensor_mul(erow[i2][:, :], erow[i2][:, :],
                                 g[f"r{i2}"][:, :])
                e.tensor_add(g["den"][:, :], erow[0][:, :], erow[1][:, :])
                e.tensor_add(g["den"][:, :], g["den"][:, :], erow[2][:, :])
                e.tensor_add(g["den"][:, :], g["den"][:, :], erow[3][:, :])
                e.reciprocal(g["rec"][:, :], g["den"][:, :])
                last = None
                for i2 in range(4):
                    last = e.tensor_mul(erow[i2][:, :], erow[i2][:, :],
                                        g["rec"][:, :])
                return last
            m_gate = sch.on("vector", f4gate, [m_gr, m_ge], "vE")
            gs("growfree", m_gate)
            gs("e4free", m_ge)
            for i2 in range(4):
                m_g4 = sch.dma("gpsimd", "dg", g4_b[i2:i2 + 1, :],
                               erow[i2][:, :],
                               waits=[m_gate, gw("g4free")] if i2 == 0 else ())
            gs("erowfree", m_g4)

            def f4g(e, c):
                return e.matmul(ps_bc[0:64, :], etab_sb[:, :], g4_b[:, :],
                                start=True, stop=True)
            m_gmm = sch.on("tensor", f4g, [m_g4, r_bc.wait(0)], "tE")
            gs("g4free", m_gmm)
            m_gmid = sch.on(
                "vector",
                lambda e, c: e.tensor_mul(gmid_b[:, :], ps_bc[0:64, :],
                                          mid_f[0:64, :]),
                [m_gmm, gw("gmidfree")], "vE")
            gs("midfree", m_gmid)
            r_bc.used(0, m_gmid)

            for oc in range(NFC):
                buf = oc % 2
                m_x2 = sch.dma("scalar", "da", xf[buf][:, :], x_fm[oc, :, tc],
                               waits=[r_xf.wait(buf)])

                def mm_d(e, c, oc=oc, buf=buf):
                    return e.matmul(ps_o[buf][:, :],
                                    bpk_sb[:, oc * 128:(oc + 1) * 128],
                                    gmid_b[:, :], start=True, stop=True)
                m_dmm = sch.on("tensor", mm_d, [m_gmid, r_pso.wait(buf)],
                               "tE")

                def uadd(e, c, oc=oc, buf=buf):
                    e.tensor_add(uf[buf][:, :], ps_o[buf][:, :],
                                 xf[buf][:, :])
                    return e.tensor_copy(xb[:, oc * TT:(oc + 1) * TT],
                                         uf[buf][:, :])
                m_ua = sch.on("vector", uadd,
                              [m_dmm, m_x2, r_uf.wait(buf)], "vE")
                r_xf.used(buf, m_ua)
                r_pso.used(buf, m_ua)
                if even:
                    m_ust = sch.dma(
                        "scalar", "da",
                        u_my.ap()[oc, :, (ti // 2) * QB:(ti // 2 + 1) * QB],
                        uf[buf][:, :], waits=[m_ua])
                else:
                    m_ust = None
                m_usq = sch.on(
                    "scalar",
                    lambda e, c, buf=buf: e.activation(
                        sqb[buf][:, :], uf[buf][:, :], AF.Square),
                    [m_ua, r_sqb.wait(buf)], "aE")
                r_uf.used(buf, m_usq if m_ust is None else m_ust)

                def mm_f5(e, c, oc=oc, buf=buf):
                    us = xb[:, oc * TT:(oc + 1) * TT]
                    e.matmul(ps_stA[:, :], ones_bf[:, 0:1], us,
                             start=(oc == 0), stop=(oc == NFC - 1))
                    return e.matmul(ps_stB[:, :], ones_bf[:, 0:1],
                                    sqb[buf][:, :], start=(oc == 0),
                                    stop=(oc == NFC - 1))
                wf5 = [m_ua, m_usq]
                if oc == 0:
                    wf5.append(r_stat.wait(0))
                m_f5 = sch.on("tensor", mm_f5, wf5, "tE")
                r_sqb.used(buf, m_f5)

            m_ln2v, m_r2 = ln_stats(m_f5)
            r_stat.used(0, m_ln2v)

            def f6bc(e, c):
                return e.matmul(ps_bc[:, :], ones_f[:, 0:128],
                                srow["rstd"][:, :], start=True, stop=True)
            m_f6 = sch.on("tensor", f6bc, [m_ln2v, m_r2, r_bc.wait(0)], "tE")
            m_r2ev = sch.on("vector",
                            lambda e, c: e.tensor_copy(r2bc[:, :],
                                                       ps_bc[:, :]),
                            [m_f6, gw("r2free")], "vE")
            r_bc.used(0, m_r2ev)

            m_rc = sch.dma("sync", "ds", rotc_sb[:, :], rotck[:, tc],
                           waits=[gw("rotuse")])
            m_rot = sch.dma("sync", "ds", rots_sb[:, :], rotsk[:, tc])
            if even:
                g2 = ti // 2
                m_rcq = sch.dma("sync", "ds", rotcq_sb[:, :],
                                rotcq[:, g2 * QB:(g2 + 1) * QB],
                                waits=[gw("rotuseq")])
                m_rotq = sch.dma("sync", "ds", rotsq_sb[:, :],
                                 rotsq[:, g2 * QB:(g2 + 1) * QB])

            for w_i, wn in enumerate(("k", "v", "q")):
                if wn == "q" and not even:
                    continue
                for oc in range(NOC):
                    buf = oc % 2
                    h, dcx = oc // 2, oc % 2
                    col = (w_i * NOC + oc) * 128
                    if ti == 0:
                        wld = sch.dma(
                            "gpsimd", "dg", wsb[buf][:, :],
                            WT[wn][oc, :, :],
                            waits=[r_wsb.wait(buf), r_wst.wait(buf)])

                        def mm_cs(e, c, buf=buf):
                            last = None
                            for fc in range(NFC):
                                last = e.matmul(
                                    ps_bc[0:1, 0:128], ones_bf[:, 0:1],
                                    wsb[buf][:, fc * 128:(fc + 1) * 128],
                                    start=(fc == 0), stop=(fc == NFC - 1))
                            return last
                        m_cmm = sch.on("tensor", mm_cs,
                                       [wld, cs_prev[0], m_r2ev], "tE")
                        m_cse = sch.on(
                            "vector",
                            lambda e, c, col=col: e.tensor_copy(
                                csW_b[:, col:col + 128], ps_bc[0:1, 0:128]),
                            [m_cmm], "vE")
                        cs_prev[0] = m_cse
                        m_wst = sch.dma("sync", "ds", w_bf.ap()[w_i, oc, :, :],
                                        wsb[buf][:, :], waits=[wld])
                        cache_mark[(w_i, oc)] = m_wst
                        r_wst.used(buf, m_wst)
                        wex = [m_cse]
                    else:
                        w0 = [r_wsb.wait(buf), r_wst.wait(buf)]
                        first_reuse = (ti == 1) or (wn == "q" and ti == 2)
                        if first_reuse:
                            w0.append(cache_mark[(w_i, oc)])
                        wld = sch.dma(
                            "gpsimd", "dg", wsb[buf][:, :],
                            w_bf.ap()[w_i, oc, :, :], waits=w0)
                        wex = []

                    def mm_p(e, c, buf=buf, oc=oc, col=col):
                        pso = ps_o[oc % 2]
                        for fc in range(NFC):
                            e.matmul(pso[:, :],
                                     wsb[buf][:, fc * 128:(fc + 1) * 128],
                                     xb[:, fc * TT:(fc + 1) * TT],
                                     start=(fc == 0), stop=False)
                        return e.matmul(pso[:, :], csW_b[:, col:col + 128],
                                        nmean_b[:, :], start=False, stop=True)
                    m_pmm = sch.on("tensor", mm_p,
                                   [wld, m_r2ev, r_pso.wait(oc % 2)] + wex,
                                   "tE")
                    r_wsb.used(buf, m_pmm)

                    kv = kvq_b[oc % 2]
                    is_rot = (wn in ("k", "q")) and (dcx == 0)
                    wkv = [r_kvq.wait(oc % 2), m_pmm]
                    if not is_rot:
                        m_ev = sch.on(
                            "vector",
                            lambda e, c, oc=oc, kv=kv: e.tensor_mul(
                                kv[:, :], ps_o[oc % 2][:, :], r2bc[:, :]),
                            wkv, "vE")
                    else:
                        def ev2(e, c, oc=oc, kv=kv):
                            e.tensor_mul(ksc[:, :], ps_o[oc % 2][0:64, :],
                                         r2bc[0:64, :])
                            return e.tensor_mul(kv[64:128, :],
                                                ps_o[oc % 2][64:128, :],
                                                r2bc[64:128, :])
                        m_evp = sch.on("vector", ev2,
                                       wkv + [gw("kscfree")], "vE")

                        def mm_swp(e, c):
                            return e.matmul(ps_aux[:, :], pi_sb[:, :],
                                            ksc[:, :], start=True, stop=True)
                        m_swp = sch.on("tensor", mm_swp,
                                       [m_evp, r_aux.wait(0),
                                        r_nrmrow.wait(0)], "tE")
                        cs_c = rotc_sb if wn == "k" else rotcq_sb
                        cs_s = rots_sb if wn == "k" else rotsq_sb
                        m_rw = m_rot if wn == "k" else m_rotq

                        def blend(e, c, kv=kv, cs_c=cs_c, cs_s=cs_s):
                            e.tensor_mul(kb1[:, :], ps_aux[:, :], cs_s[:, :])
                            e.tensor_mul(ksc[:, :], ksc[:, :], cs_c[:, :])
                            return e.tensor_add(kv[0:64, :], kb1[:, :],
                                                ksc[:, :])
                        m_ev = sch.on("vector", blend, [m_swp, m_rw], "vE")
                        gs("kscfree", m_ev)
                        r_aux.used(0, m_ev)
                    r_pso.used(oc % 2, m_ev)

                    if wn == "k":
                        m_st = sch.dma("sync", "ds", k_fm.ap()[h, dcx, :, tc],
                                       kv[:, :], waits=[m_ev])
                        r_kvq.used(oc % 2, m_st)
                    elif wn == "q":
                        g2 = ti // 2
                        m_st = sch.dma("sync", "ds",
                                       q_my.ap()[h, dcx, :,
                                                 g2 * QB:(g2 + 1) * QB],
                                       kv[:, :], waits=[m_ev])
                        r_kvq.used(oc % 2, m_st)
                    else:
                        def vtr(e, c, kv=kv):
                            last = None
                            for rb in range(4):
                                last = e.transpose(
                                    ps_vtb[:, rb * 128:(rb + 1) * 128],
                                    kv[:, rb * 128:(rb + 1) * 128],
                                    ident_sb[:, :])
                            return last
                        m_tr = sch.on("tensor", vtr,
                                      [m_ev, r_vtb.wait(0)], "tE")
                        r_kvq.used(oc % 2, m_tr)

                        def vev(e, c, oc=oc):
                            dst = vrow.rearrange("p (rb cc) -> p rb cc",
                                                 cc=DC)
                            return e.tensor_copy(
                                dst[:, :, oc * 128:(oc + 1) * 128],
                                ps_vtb.rearrange("p (rb cc) -> p rb cc",
                                                 cc=128))
                        m_vev = sch.on("vector", vev,
                                       [m_tr, r_vrow.wait(0)
                                        if oc == 0 else None], "vE")
                        r_vtb.used(0, m_vev)

                    if wn in ("k", "q"):
                        m_sqn = sch.on(
                            "scalar",
                            lambda e, c, kv=kv, buf=buf: e.activation(
                                sqb[buf][:, :], kv[:, :], AF.Square),
                            [m_ev, r_sqb.wait(buf)], "aE")

                        def nmm(e, c, buf=buf, dcx=dcx):
                            return e.matmul(ps_aux[0:1, :], ones_bf[:, 0:1],
                                            sqb[buf][:, :],
                                            start=(dcx == 0), stop=(dcx == 1))
                        wn2 = [m_sqn]
                        if dcx == 0:
                            wn2 += [r_nrmrow.wait(0), r_aux.wait(0)]
                        m_nmm = sch.on("tensor", nmm, wn2, "tE")
                        r_sqb.used(buf, m_nmm)
                        if dcx == 1:
                            tgt = qmax_sb if wn == "q" else kmax_sb

                            def nred(e, c, h=h, tgt=tgt):
                                e.reduce_max(nrm1[:, :], ps_aux[0:1, :],
                                             axis=AX.X)
                                return e.tensor_max(tgt[:, h:h + 1],
                                                    tgt[:, h:h + 1],
                                                    nrm1[:, :])
                            m_nr = sch.on("vector", nred, [m_nmm], "vE")
                            r_nrmrow.used(0, m_nr)
                if wn == "v":
                    src4 = vrow.rearrange("p (rb hh d) -> p rb hh d",
                                          rb=4, hh=NH)
                    for h2 in range(NH):
                        m_vst = sch.dma(
                            "sync", "ds",
                            v_tm.ap()[h2, ti * TT:(ti + 1) * TT, :]
                            .rearrange("(rb p) d -> p rb d", p=128),
                            src4[:, :, h2, :],
                            waits=[m_vev] if h2 == 0 else ())
                    r_vrow.used(0, m_vst)
            if ti == 0:
                r_bc.used(0, cs_prev[0])
            gs("xbuse", m_pmm)
            gs("r2free", m_pmm)
            gs("gmidfree", ("vE", sch.val("vE")))
            gs("rotuse", ("vE", sch.val("vE")))
            if even:
                gs("rotuseq", ("vE", sch.val("vE")))

        fe.close()

        def biasf(e, c):
            e.tensor_mul(bias_sb[0:1, 0:NH], qmax_sb[:, :], kmax_sb[:, :])
            return None
        m_b1 = sch.on("vector", biasf, [("vE", sch.val("vE")),
                                        ("tE", sch.val("tE"))], "vE")
        m_b2 = sch.on("scalar",
                      lambda e, c: e.activation(bias_sb[0:1, 0:NH],
                                                bias_sb[0:1, 0:NH], AF.Sqrt,
                                                scale=1.0 / 256.0),
                      [m_b1], "aE")
        m_b3 = sch.on("vector",
                      lambda e, c: e.tensor_scalar_mul(bias_sb[0:1, 0:NH],
                                                       bias_sb[0:1, 0:NH],
                                                       -1.0),
                      [m_b2], "vE")
        bs = ExitStack()
        ps_bb = bs.enter_context(nc.psum_tensor("ps_bb", [128, NH], F32))
        brow = bs.enter_context(nc.sbuf_tensor("brow", [1, NH], F32))
        m_b4 = sch.on("vector",
                      lambda e, c: e.tensor_copy(brow[:, :],
                                                 bias_sb[0:1, 0:NH]),
                      [m_b3], "vE")

        def bbc(e, c):
            return e.matmul(ps_bb[:, :], ones_f[:, 0:128], brow[:, :],
                            start=True, stop=True)
        m_b5 = sch.on("tensor", bbc, [m_b4], "tE")
        m_bias = sch.on("vector",
                        lambda e, c: e.tensor_copy(bias_sb[:, :],
                                                   ps_bb[:, :]),
                        [m_b5], "vE")
        bs.close()

        if dbg == 1:
            sch.dma("sync", "ds", nrm_dbg.ap()[0:1, :], qmax_sb[:, :],
                    waits=[m_bias, ("da", sch.val("da")),
                           ("dg", sch.val("dg"))])
            m_fin = sch.dma("sync", "ds", nrm_dbg.ap()[1:2, :], kmax_sb[:, :])
            sch.on("sync", lambda e, c: None, [("ds", sch.val("ds"))])
            _emit(nc, sch, semh, ectx)
            return nc

        at = ExitStack()
        Ea = at.enter_context
        ksl = [Ea(nc.sbuf_tensor(f"ksl{i}", [128, 2 * T], BF16))
               for i in range(2)]
        vsl = [Ea(nc.sbuf_tensor(f"vsl{i}", [128, 64 * HD], BF16))
               for i in range(2)]
        qsl = [Ea(nc.sbuf_tensor(f"qsl{i}", [128, 2 * NQB * QB], BF16))
               for i in range(2)]
        p_sb = [[Ea(nc.sbuf_tensor(f"p_sb{a}{b}", [128, QB], BF16))
                 for b in range(2)] for a in range(2)]
        lrow = [Ea(nc.sbuf_tensor(f"lrow{a}", [1, QB], F32)) for a in range(2)]
        rrow = [Ea(nc.sbuf_tensor(f"rrow{a}", [1, QB], F32)) for a in range(2)]
        recbc = Ea(nc.sbuf_tensor("recbc", [128, QB], F32))
        obf = Ea(nc.sbuf_tensor("obf", [128, QB], BF16))

        ps_sc = [Ea(nc.psum_tensor(f"ps_sc{a}", [128, QB], F32))
                 for a in range(2)]
        ps_O = [[Ea(nc.psum_tensor(f"ps_O{a}{d2}", [128, QB], F32))
                 for d2 in range(2)] for a in range(2)]
        ps_rs = [Ea(nc.psum_tensor(f"ps_rs{a}", [1, QB], F32))
                 for a in range(2)]

        r_ksl = Roll()
        r_vsl = Roll()
        r_qsl = Roll()
        r_pbuf = {}

        def g_regs(e, c):
            c["p16"] = e.alloc_register("p16")
            e.reg_load(c["p16"], offs_sb[0:1, 0:1])
            c["p16v"] = e.snap(e.compute_val(c["p16"]), min_val=0, max_val=16)
            return None
        sch.on("gpsimd", g_regs, [m_cons[6]])

        for h in range(NH):
            hb = h % 2
            m_kl = sch.dma("sync", "ds",
                           ksl[hb].rearrange("p (d t) -> p d t", t=T),
                           k_fm.ap()[h, :, :, :].rearrange("d p t -> p d t"),
                           waits=[r_ksl.wait(hb), ("vE", sch.val("vE"))
                                  if h == 0 else None])
            m_vl0 = None
            for part in range(4):
                m_vl = sch.dma(
                    "sync", "ds",
                    vsl[hb].rearrange("p (kc d) -> p kc d", d=HD)
                    [:, part * 16:(part + 1) * 16, :],
                    v_tm.ap()[h, part * 2048:(part + 1) * 2048, :]
                    .rearrange("(kc p) d -> p kc d", p=128),
                    waits=[r_vsl.wait(hb)] if part == 0 else ())
            m_ql = sch.dma("sync", "ds",
                           qsl[hb].rearrange("p (d q) -> p d q", q=NQB * QB),
                           q_my.ap()[h, :, :, :].rearrange("d p q -> p d q"),
                           waits=[r_qsl.wait(hb)])

            for qbp in range(4):
                qA, qAn = 2 * qbp, 16 * qbp + 8
                qBi, qBn = 2 * qbp + 1, 16 * qbp + 16
                exp_mark = {}
                qk_mark = {}
                pfin_mark = {}
                for kc in range(qBn):
                    valid = ([(0, qA, qAn)] if kc < qAn else []) + \
                        [(1, qBi, qBn)]
                    def mm_qk(e, c, kc=kc, valid=tuple(valid), h=h, hb=hb):
                        last = None
                        for d2 in range(2):
                            for (sl, qb, _) in valid:
                                last = e.matmul(
                                    ps_sc[sl][:, :],
                                    ksl[hb][:, d2 * T + kc * 128:
                                            d2 * T + (kc + 1) * 128],
                                    qsl[hb][:, d2 * NQB * QB + qb * QB:
                                            d2 * NQB * QB + (qb + 1) * QB],
                                    start=(d2 == 0), stop=(d2 == 1))
                        return last
                    wqk = [m_kl, m_ql]
                    for (sl, qb, qn) in valid:
                        wqk.append(exp_mark.get((sl, kc - 1)))
                    if kc == 0:
                        wqk.append(gw(("scfree", 0)))
                        wqk.append(gw(("scfree", 1)))
                        wqk.append(m_bias)
                    qk_mark[kc] = sch.on("tensor", mm_qk, wqk, "tE")

                    for (sl, qb, qn) in valid:
                        pt = p_sb[sl][kc % 2]

                        def expf(e, c, sl=sl, pt=pt, h=h):
                            return e.activation(pt[:, :], ps_sc[sl][:, :],
                                                AF.Exp,
                                                bias=bias_sb[:, h:h + 1],
                                                scale=1.0 / 16.0)
                        we = [qk_mark[kc], r_pbuf.get((sl, kc % 2))]
                        exp_mark[(sl, kc)] = sch.on("scalar", expf, we, "aE")
                        pfin_mark[(sl, kc)] = exp_mark[(sl, kc)]
                        rel = kc - (qn - 8)
                        if 0 <= rel < 8:
                            def maskf(e, c, sl=sl, pt=pt, rel=rel):
                                return e.tensor_mul(
                                    pt[:, :], pt[:, :],
                                    mask_sb[:, rel * QB:(rel + 1) * QB])
                            pfin_mark[(sl, kc)] = sch.on(
                                "vector", maskf, [exp_mark[(sl, kc)]], "vE")

                    def emit_pv(kc2):
                        vald2 = ([(0, qA, qAn)] if kc2 < qAn else []) + \
                            [(1, qBi, qBn)]

                        def mm_pv(e, c, kc2=kc2, vald2=tuple(vald2), hb=hb):
                            last = None
                            for d2 in range(2):
                                for (sl, qb, qn) in vald2:
                                    last = e.matmul(
                                        ps_O[sl][d2][:, :],
                                        vsl[hb][:, kc2 * HD + d2 * 128:
                                                kc2 * HD + (d2 + 1) * 128],
                                        p_sb[sl][kc2 % 2][:, :],
                                        start=(kc2 == 0),
                                        stop=(kc2 == qn - 1))
                            for (sl, qb, qn) in vald2:
                                last = e.matmul(
                                    ps_rs[sl][:, :], ones_bf[:, 0:1],
                                    p_sb[sl][kc2 % 2][:, :],
                                    start=(kc2 == 0), stop=(kc2 == qn - 1))
                            return last
                        wpv = [m_vl, pfin_mark[(0, kc2)]
                               if kc2 < qAn else None, pfin_mark[(1, kc2)]]
                        if kc2 == 0:
                            wpv += [gw(("ofree", 0)), gw(("ofree", 1)),
                                    gw(("rsfree", 0)), gw(("rsfree", 1))]
                        pv = sch.on("tensor", mm_pv, wpv, "tE")
                        for (sl, qb, qn) in vald2:
                            r_pbuf[(sl, kc2 % 2)] = pv
                        return pv
                    if kc >= 1:
                        emit_pv(kc - 1)
                m_pvlast = emit_pv(qBn - 1)

                for sl, qb, qn in ((0, qA, qAn), (1, qBi, qBn)):
                    m_rsv = sch.on(
                        "vector",
                        lambda e, c, sl=sl: e.tensor_copy(lrow[sl][:, :],
                                                          ps_rs[sl][:, :]),
                        [m_pvlast], "vE")
                    m_rec = sch.on(
                        "vector",
                        lambda e, c, sl=sl: e.reciprocal(rrow[sl][:, :],
                                                         lrow[sl][:, :]),
                        [m_rsv], "vE")
                    gs(("rsfree", sl), m_rsv)

                    def recbcf(e, c, sl=sl):
                        return e.matmul(ps_sc[sl][:, :], ones_f[:, 0:128],
                                        rrow[sl][:, :], start=True, stop=True)
                    m_rbc = sch.on("tensor", recbcf,
                                   [m_rec, exp_mark[(sl, qn - 1)]], "tE")
                    m_rbe = sch.on(
                        "vector",
                        lambda e, c, sl=sl: e.tensor_copy(recbc[:, :],
                                                          ps_sc[sl][:, :]),
                        [m_rbc, gw("recbcuse")], "vE")
                    gs(("scfree", sl), m_rbe)
                    for d2 in range(2):
                        m_on = sch.on(
                            "vector",
                            lambda e, c, sl=sl, d2=d2: e.tensor_mul(
                                obf[:, :], ps_O[sl][d2][:, :], recbc[:, :]),
                            [m_rbe, gw("obffree")], "vE")
                        gs("recbcuse", m_on)
                        gs(("ofree", sl), m_on)

                        rr = (2 * h + d2) * 128
                        m_ost = sch.dma(
                            "scalar", "da",
                            o_loc.ap()[rr:rr + 128, qb * QB:(qb + 1) * QB],
                            obf[:, :], waits=[m_on])
                        gs("obffree", m_ost)
            r_ksl.used(hb, qk_mark[qBn - 1])
            r_vsl.used(hb, m_pvlast)
            r_qsl.used(hb, qk_mark[qBn - 1])

        at.close()

        def o_bulk(e, c):
            half = e.snap(c["p16v"] // 16, min_val=0, max_val=1)
            return e.dma_start(out=sh_O.ap()[bass.ts(half, DC), :],
                               in_=o_loc.ap())
        m_obulk = sch.on("gpsimd", o_bulk,
                         [("ds", sch.val("ds")), ("da", sch.val("da"))],
                         "gE", 16)

        def flagw(e, c):
            c["myslot"] = e.alloc_register("myslot")
            c["paslot"] = e.alloc_register("paslot")
            c["nonce"] = e.alloc_register("nonce")
            e.reg_load(c["myslot"], offs_sb[0:1, 5:6])
            e.reg_load(c["paslot"], offs_sb[0:1, 6:7])
            e.reg_load(c["nonce"], offs_sb[0:1, 7:8])
            c["mysv"] = e.snap(e.compute_val(c["myslot"]), min_val=0,
                               max_val=1)
            c["pasv"] = e.snap(e.compute_val(c["paslot"]), min_val=0,
                               max_val=1)
            return None
        sch.on("gpsimd", flagw, [m_obulk])
        m_fw0 = sch.dma("gpsimd", "dg", flagw_sb[:, :], offs)
        def flag_store(e, c):
            return e.dma_start(out=sh_flag.ap()[bass.ts(c["mysv"], 1), :],
                               in_=flagw_sb[0:1, :])
        m_fst = sch.on("gpsimd", flag_store, [m_fw0], "gE", 16)

        def poll(e, c):
            with e.register("cond") as cond, e.register("tgt") as tgt, \
                    e.register("v") as v, e.register("ni") as ni, \
                    e.register("t2") as t2:
                e.reg_mov(cond, 1)
                e.reg_mov(ni, 0)
                e.reg_mov(tgt, sch.val("pp"))
                with e.While(cond):
                    e.dma_start(out=flagr_sb[:, :],
                                in_=sh_flag.ap()[bass.ts(c["pasv"], 1), :]
                                ).then_inc(semh["pp"], 16)
                    e.reg_add(tgt, tgt, 16)
                    e.wait_ge(semh["pp"], tgt)
                    e.reg_load(v, flagr_sb[0:1, 7:8])
                    e.reg_add(ni, ni, 1)
                    e.reg_sub(cond, v, c["nonce"])
                    e.reg_sub(t2, NPOLL, ni)
                    with e.If_lt(t2, 1):
                        e.reg_mov(cond, 0)
            return None
        m_poll = sch.on("gpsimd", poll, [m_fst], "gE", 1)

        if dbg == 2:
            m_cp = sch.dma("gpsimd", "dg", o_dbg.ap(), sh_O.ap(),
                           waits=[m_poll])
            sch.on("gpsimd", lambda e, c: None, [("dg", sch.val("dg"))])
            _emit(nc, sch, semh, ectx)
            return nc

        be = ExitStack()
        Eb = be.enter_context
        osb = [Eb(nc.sbuf_tensor(f"osb{b}", [128, NFC * QB], BF16))
               for b in range(4)]
        wob = [Eb(nc.sbuf_tensor(f"wob{i}", [128, D], BF16)) for i in range(2)]
        ub = [Eb(nc.sbuf_tensor(f"ub{i}", [128, QB], F32)) for i in range(2)]
        vf = [Eb(nc.sbuf_tensor(f"vf{i}", [128, QB], F32)) for i in range(2)]
        vb = Eb(nc.sbuf_tensor("vb", [128, QB], BF16))
        ps_vo = [Eb(nc.psum_tensor(f"ps_vo{i}", [128, QB], F32))
                 for i in range(4)]

        def osb_regs(e, c):
            for bb2 in range(4):
                r = e.alloc_register(f"uc{bb2}")
                e.reg_load(r, offs_sb[0:1, 1 + bb2:2 + bb2])
                c[f"ucv{bb2}"] = e.snap(e.compute_val(r), min_val=0,
                                        max_val=7)
            return None
        sch.on("gpsimd", osb_regs, [m_poll])
        m_usel = []
        for bb in range(4):
            def u_cp(e, c, bb=bb):
                return e.dma_start(
                    out=u_sel.ap()[:, :, bb * QB:(bb + 1) * QB],
                    in_=u_my.ap()[:, :, bass.ts(c[f"ucv{bb}"], QB)])
            m_usel.append(sch.on("gpsimd", u_cp, [m_poll], "gE", 16))
        m_old = []
        for bb in range(4):
            def o_ld(e, c, bb=bb):
                return e.dma_start(
                    out=osb[bb].rearrange("p (fc q) -> p fc q", q=QB),
                    in_=sh_O.ap()[:, bass.ts(c[f"ucv{bb}"], QB)]
                    .rearrange("(fc p) q -> p fc q", p=128))
            m_old.append(sch.on("gpsimd", o_ld, [m_poll], "gE", 16))

        r_wob = Roll()
        r_ub = Roll()
        r_vf = Roll()
        r_vb = Roll(1)
        r_vops = [Roll(1) for _ in range(4)]

        for ec in range(NFC):
            buf = ec % 2
            wld = sch.dma("gpsimd", "dg", wob[buf][:, :],
                          wo_t[ec, :, :],
                          waits=[r_wob.wait(buf)])

            def mm_wo(e, c, buf=buf):
                last = None
                for dcc in range(NFC):
                    for bb in range(4):
                        last = e.matmul(ps_vo[bb][:, :],
                                        wob[buf][:, dcc * 128:(dcc + 1) * 128],
                                        osb[bb][:, dcc * QB:(dcc + 1) * QB],
                                        start=(dcc == 0),
                                        stop=(dcc == NFC - 1))
                return last
            wvo = [wld] + m_old + [r_vops[bb].wait(0) for bb in range(4)]
            m_vo = sch.on("tensor", mm_wo, wvo, "tE")
            r_wob.used(buf, m_vo)

            for bb in range(4):
                m_ul = sch.dma(
                    "scalar", "da", ub[buf][:, :],
                    u_sel.ap()[ec, :, bb * QB:(bb + 1) * QB],
                    waits=[r_ub.wait(buf), m_usel[bb]])

                def vadd(e, c, bb=bb, buf=buf):
                    return e.tensor_add(vf[buf][:, :], ps_vo[bb][:, :],
                                        ub[buf][:, :])
                m_va = sch.on("vector", vadd, [m_vo, m_ul, r_vf.wait(buf)],
                              "vE")
                r_ub.used(buf, m_va)
                r_vops[bb].used(0, m_va)
                m_vst = sch.dma("sync", "ds",
                                v_my.ap()[ec, :, bb * QB:(bb + 1) * QB],
                                vf[buf][:, :], waits=[m_va])
                m_vbc = sch.on(
                    "vector",
                    lambda e, c, buf=buf: e.tensor_copy(vb[:, :],
                                                        vf[buf][:, :]),
                    [m_va, r_vb.wait(0)], "vE")
                r_vf.used(buf, m_vst)
                m_vbst = sch.dma("sync", "ds",
                                 vbf_h.ap()[ec, :, bb * QB:(bb + 1) * QB],
                                 vb[:, :], waits=[m_vbc])
                r_vb.used(0, m_vbst)
        m_wo_done = ("ds", sch.val("ds"))
        be.close()

        st_out = []
        for half in range(2):
            TB2 = 2 * QB
            hc = slice(half * TB2, (half + 1) * TB2)
            hs = ExitStack()
            E2 = hs.enter_context
            vbs = [E2(nc.sbuf_tensor(f"vbs{half}{i}", [128, TB2], BF16))
                   for i in range(2)]
            sq2b = [E2(nc.sbuf_tensor(f"sq2b{half}{i}", [128, TB2], BF16))
                    for i in range(2)]
            mid2 = E2(nc.sbuf_tensor(f"mid2_{half}", [68, TB2], F32))
            e42 = E2(nc.sbuf_tensor(f"e42_{half}", [4, TB2], F32))
            g42 = E2(nc.sbuf_tensor(f"g42_{half}", [4, TB2], BF16))
            gmid2 = E2(nc.sbuf_tensor(f"gmid2_{half}", [64, TB2], BF16))
            r3bc = E2(nc.sbuf_tensor(f"r3bc_{half}", [68, TB2], F32))
            srow2 = {k: E2(nc.sbuf_tensor(f"s2{half}_{k}", [1, TB2], F32))
                     for k in RK}
            nmean2 = E2(nc.sbuf_tensor(f"nmean2_{half}", [1, TB2], BF16))
            grow2 = {k: E2(nc.sbuf_tensor(f"g2{half}_{k}", [1, TB2], F32))
                     for k in GK}
            erow2 = [E2(nc.sbuf_tensor(f"e2{half}_{i}", [1, TB2], F32))
                     for i in range(4)]
            vrd = [E2(nc.sbuf_tensor(f"vrd{half}{i}", [128, QB], F32))
                   for i in range(2)]
            of = [E2(nc.sbuf_tensor(f"of{half}{i}", [128, QB], F32))
                  for i in range(2)]

            ps_b2 = E2(nc.psum_tensor(f"ps_b2_{half}", [68, TB2], F32))
            sub = ExitStack()
            ps_m2 = sub.enter_context(
                nc.psum_tensor(f"ps_m2_{half}", [68, TB2], F32))
            ps_s3A = sub.enter_context(
                nc.psum_tensor(f"ps_s3A_{half}", [1, TB2], F32))
            ps_s3B = sub.enter_context(
                nc.psum_tensor(f"ps_s3B_{half}", [1, TB2], F32))

            r_vbs = Roll()
            r_sq2 = Roll()
            for fc in range(NFC):
                buf = fc % 2
                wv2 = [r_vbs.wait(buf)]
                if fc <= 1 and half == 0:
                    wv2.append(m_wo_done)
                m_vl2 = sch.dma("sync", "ds", vbs[buf][:, :],
                                vbf_h.ap()[fc, :, hc], waits=wv2)
                m_sq2 = sch.on(
                    "scalar",
                    lambda e, c, buf=buf: e.activation(
                        sq2b[buf][:, :], vbs[buf][:, :], AF.Square),
                    [m_vl2, r_sq2.wait(buf)], "aE")

                def mm_m2(e, c, fc=fc, buf=buf):
                    for ts2 in range(2):
                        e.matmul(ps_m2[:, ts2 * QB:(ts2 + 1) * QB],
                                 ast_sb[:, fc * 68:(fc + 1) * 68],
                                 vbs[buf][:, ts2 * QB:(ts2 + 1) * QB],
                                 start=(fc == 0), stop=False)
                    for ts2 in range(2):
                        e.matmul(ps_s3A[:, ts2 * QB:(ts2 + 1) * QB],
                                 ones_bf[:, 0:1],
                                 vbs[buf][:, ts2 * QB:(ts2 + 1) * QB],
                                 start=(fc == 0), stop=(fc == NFC - 1))
                    last = None
                    for ts2 in range(2):
                        last = e.matmul(ps_s3B[:, ts2 * QB:(ts2 + 1) * QB],
                                        ones_bf[:, 0:1],
                                        sq2b[buf][:, ts2 * QB:(ts2 + 1) * QB],
                                        start=(fc == 0), stop=(fc == NFC - 1))
                    return last
                m_m2 = sch.on("tensor", mm_m2, [m_vl2, m_sq2], "tE")
                r_vbs.used(buf, m_m2)
                r_sq2.used(buf, m_m2)

            def f23(e, c):
                e.tensor_scalar_mul(srow2["mean"][:, :], ps_s3A[:, :],
                                    1.0 / D)
                e.tensor_scalar_mul(srow2["tmp"][:, :], ps_s3B[:, :], 1.0 / D)
                e.tensor_mul(srow2["var"][:, :], srow2["mean"][:, :],
                             srow2["mean"][:, :])
                e.tensor_sub(srow2["var"][:, :], srow2["tmp"][:, :],
                             srow2["var"][:, :])
                e.tensor_scalar_add(srow2["var"][:, :], srow2["var"][:, :],
                                    EPS)
                e.tensor_scalar_mul(srow2["nmean"][:, :], srow2["mean"][:, :],
                                    -1.0)
                return e.tensor_copy(nmean2[:, :], srow2["nmean"][:, :])
            m_l3v = sch.on("vector", f23, [m_m2], "vE")

            def f23r(e, c):
                e.activation(srow2["tmp"][:, :], srow2["var"][:, :], AF.Ln)
                return e.activation(srow2["rstd"][:, :], srow2["tmp"][:, :],
                                    AF.Exp, scale=-0.5)
            m_l3r = sch.on("scalar", f23r, [m_l3v], "aE")

            def m2aug(e, c):
                for ts2 in range(2):
                    e.matmul(ps_m2[:, ts2 * QB:(ts2 + 1) * QB], csA_b[:, :],
                             nmean2[:, ts2 * QB:(ts2 + 1) * QB],
                             start=False, stop=True)
                last = None
                for ts2 in range(2):
                    last = e.matmul(ps_b2[0:68, ts2 * QB:(ts2 + 1) * QB],
                                    ones_f[:, 0:68],
                                    srow2["rstd"][:, ts2 * QB:(ts2 + 1) * QB],
                                    start=True, stop=True)
                return last
            m_m2a = sch.on("tensor", m2aug, [m_l3v, m_l3r], "tE")

            def m2ev(e, c):
                e.tensor_copy(r3bc[:, :], ps_b2[0:68, :])
                return e.tensor_mul(mid2[:, :], ps_m2[:, :], r3bc[:, :])
            m_m2e = sch.on("vector", m2ev, [m_m2a], "vE")

            for i2 in range(4):
                m_gr2 = sch.dma("gpsimd", "dg", grow2[f"r{i2}"][:, :],
                                mid2[64 + i2:65 + i2, :],
                                waits=[m_m2e] if i2 == 0 else ())
            m_e42 = sch.on("scalar",
                           lambda e, c: e.activation(e42[:, :],
                                                     mid2[64:68, :], AF.Exp),
                           [m_m2e], "aE")
            for i2 in range(4):
                m_ge2 = sch.dma("gpsimd", "dg", erow2[i2][:, :],
                                e42[i2:i2 + 1, :],
                                waits=[m_e42] if i2 == 0 else ())

            def f4gate2(e, c):
                g = grow2
                e.tensor_max(g["w0"][:, :], g["r0"][:, :], g["r1"][:, :])
                e.tensor_tensor(g["M"][:, :], g["r0"][:, :], g["r1"][:, :],
                                ALU.min)
                e.tensor_max(g["w1"][:, :], g["r2"][:, :], g["r3"][:, :])
                e.tensor_tensor(g["T2"][:, :], g["r2"][:, :], g["r3"][:, :],
                                ALU.min)
                e.tensor_max(g["den"][:, :], g["M"][:, :], g["T2"][:, :])
                e.tensor_tensor(g["T2"][:, :], g["w0"][:, :], g["w1"][:, :],
                                ALU.min)
                e.tensor_max(g["T2"][:, :], g["T2"][:, :], g["den"][:, :])
                for i2 in range(4):
                    e.tensor_tensor(g[f"r{i2}"][:, :], g[f"r{i2}"][:, :],
                                    g["T2"][:, :], ALU.is_ge)
                for i2 in range(4):
                    e.tensor_mul(erow2[i2][:, :], erow2[i2][:, :],
                                 g[f"r{i2}"][:, :])
                e.tensor_add(g["den"][:, :], erow2[0][:, :], erow2[1][:, :])
                e.tensor_add(g["den"][:, :], g["den"][:, :], erow2[2][:, :])
                e.tensor_add(g["den"][:, :], g["den"][:, :], erow2[3][:, :])
                e.reciprocal(g["rec"][:, :], g["den"][:, :])
                last = None
                for i2 in range(4):
                    last = e.tensor_mul(erow2[i2][:, :], erow2[i2][:, :],
                                        g["rec"][:, :])
                return last
            m_gate2 = sch.on("vector", f4gate2, [m_gr2, m_ge2], "vE")
            for i2 in range(4):
                m_g42 = sch.dma("gpsimd", "dg", g42[i2:i2 + 1, :],
                                erow2[i2][:, :],
                                waits=[m_gate2] if i2 == 0 else ())

            def f4g2(e, c):
                last = None
                for ts2 in range(2):
                    last = e.matmul(ps_b2[0:64, ts2 * QB:(ts2 + 1) * QB],
                                    etab_sb[:, :],
                                    g42[:, ts2 * QB:(ts2 + 1) * QB],
                                    start=True, stop=True)
                return last
            m_gmm2 = sch.on("tensor", f4g2, [m_g42, m_m2e], "tE")
            m_gmid2 = sch.on(
                "vector",
                lambda e, c: e.tensor_mul(gmid2[:, :], ps_b2[0:64, :],
                                          mid2[0:64, :]),
                [m_gmm2], "vE")

            sub.close()
            ps_d2 = [E2(nc.psum_tensor(f"ps_d2_{half}{i}", [128, QB], F32))
                     for i in range(2)]
            r_d2 = Roll()
            r_vrd = Roll()
            r_of = Roll()
            for oc in range(NFC):
                for ts2 in range(2):
                    buf = (oc * 2 + ts2) % 2
                    colg = half * TB2 + ts2 * QB

                    def mm_d2(e, c, oc=oc, ts2=ts2, buf=buf):
                        return e.matmul(ps_d2[buf][:, :],
                                        bpk_sb[:, oc * 128:(oc + 1) * 128],
                                        gmid2[:, ts2 * QB:(ts2 + 1) * QB],
                                        start=True, stop=True)
                    m_d2 = sch.on("tensor", mm_d2,
                                  [m_gmid2, r_d2.wait(buf)], "tE")
                    m_vrd = sch.dma("scalar", "da", vrd[buf][:, :],
                                    v_my.ap()[oc, :, colg:colg + QB],
                                    waits=[r_vrd.wait(buf)])

                    def oadd(e, c, buf=buf):
                        return e.tensor_add(of[buf][:, :], ps_d2[buf][:, :],
                                            vrd[buf][:, :])
                    m_oa = sch.on("vector", oadd,
                                  [m_d2, m_vrd, r_of.wait(buf)], "vE")
                    r_d2.used(buf, m_oa)
                    r_vrd.used(buf, m_oa)
                    m_ost = sch.dma("sync", "ds",
                                    out_my.ap()[oc, :, colg:colg + QB],
                                    of[buf][:, :], waits=[m_oa])
                    r_of.used(buf, m_ost)
                    st_out.append(m_ost)
            hs.close()

        sch.on("sync", lambda e, c: None, [("ds", sch.val("ds")),
                                           ("da", sch.val("da")),
                                           ("dg", sch.val("dg")),
                                           ("gE", sch.val("gE"))])
        _emit(nc, sch, semh, ectx)
    return nc



def _perm(role):
    idx = np.arange(T).reshape(8, 2, 512)
    if role == 1:
        idx = idx[:, ::-1, :]
    return idx.reshape(-1)


def _build_inputs(inputs, nonce):
    x = np.asarray(inputs["L0_attn_hidden_states"])
    Wq = np.asarray(inputs["L0_attn_Wq"])
    Wk = np.asarray(inputs["L0_attn_Wk"])
    Wv = np.asarray(inputs["L0_attn_Wv"])
    Wo = np.asarray(inputs["L0_attn_Wo"])
    A = np.asarray(inputs["L0_ffn_A"])
    B = np.asarray(inputs["L0_ffn_B"])
    Wr = np.asarray(inputs["L0_ffn_Wr"])

    invf = 1.0 / (10000.0 ** (np.arange(0, 64, 2, dtype=np.float64) / 64.0))
    freqs = invf[np.arange(64) // 2]

    wo_t = np.ascontiguousarray(
        Wo.reshape(NFC, 128, NFC, 128).transpose(2, 1, 0, 3)
        .reshape(NFC, 128, D)).astype(np.float32)
    a_stack = np.concatenate(
        [A.transpose(2, 0, 1).reshape(D, 64), Wr.T], axis=1)
    a_st = np.ascontiguousarray(
        a_stack.reshape(NFC, 128, 68)).astype(np.float32)
    b_pk = np.ascontiguousarray(
        B.transpose(0, 2, 1).reshape(64, D)).astype(np.float32)
    e_tab = np.zeros((4, 64), dtype=np.float32)
    for ee in range(4):
        e_tab[ee, ee * 16:(ee + 1) * 16] = 2.0
    pi_tab = np.zeros((64, 64), dtype=np.float32)
    for j in range(64):
        pi_tab[j ^ 1, j] = 1.0
    identf = np.eye(128, dtype=np.float32)

    w_sl = {}
    for par in range(2):
        sl = slice(par * DC, (par + 1) * DC)
        for nm, W in (("wq_t", Wq), ("wk_t", Wk), ("wv_t", Wv)):
            w_sl[(nm, par)] = np.ascontiguousarray(
                np.asarray(W)[:, sl].reshape(NFC, 128, NOC, 128)
                .transpose(2, 1, 0, 3).reshape(NOC, 128, D)
            ).astype(np.float32)

    in_maps = []
    for core in range(8):
        b = core // 4
        role = (core % 4) // 2
        par = core % 2
        perm = _perm(role)
        x_fm = np.ascontiguousarray(
            np.asarray(x)[b].T[:, perm].reshape(NFC, 128, T)
        ).astype(np.float32)
        ang_k = freqs[:, None] * perm[None, :].astype(np.float64)
        posq = (np.arange(NQB) * 1024 + role * 512)[:, None] \
            + np.arange(QB)[None, :]
        posq = posq.reshape(-1).astype(np.float64)
        ang_q = freqs[:, None] * posq[None, :]
        maskbk = np.zeros((8, 128, QB), dtype=np.float32)
        p_i = np.arange(128)[:, None]
        f_i = np.arange(QB)[None, :]
        for j in range(4):
            maskbk[j] = (f_i >= p_i + 128 * j).astype(np.float32)
        if role == 1:
            maskbk[4:8] = 1.0
        off = np.zeros((1, 16), dtype=np.int32)
        off[0, 0] = par * 16
        for bb in range(4):
            off[0, 1 + bb] = (par + 2 * bb)
        off[0, 5] = par
        off[0, 6] = 1 - par
        off[0, 7] = nonce
        in_maps.append({
            "x_fm": x_fm,
            "wq_t": w_sl[("wq_t", par)],
            "wk_t": w_sl[("wk_t", par)],
            "wv_t": w_sl[("wv_t", par)],
            "wo_t": wo_t,
            "a_st": a_st, "b_pk": b_pk, "e_tab": e_tab, "pi_tab": pi_tab,
            "rotck": np.cos(ang_k).astype(np.float32),
            "rotsk": np.sin(ang_k).astype(np.float32),
            "rotcq": np.cos(ang_q).astype(np.float32),
            "rotsq": np.sin(ang_q).astype(np.float32),
            "maskb": maskbk, "identf": identf, "offs": off,
        })
    return in_maps


def _assemble(results):
    out = np.zeros((2, T, D), dtype=np.float32)
    for core in range(8):
        b = core // 4
        role = (core % 4) // 2
        par = core % 2
        o = results[core]["out_my"].reshape(D, 4 * QB)
        for bb in range(4):
            g = par + 2 * bb
            t0 = 1024 * g + 512 * role
            out[b, t0:t0 + QB, :] = o[:, bb * QB:(bb + 1) * QB].T
    return out


_NONCE = [12345]
_NC_CACHE = {}


def kernel(**inputs):
    _NONCE[0] += 1
    in_maps = _build_inputs(inputs, _NONCE[0] * 7919 + 13)
    if 0 not in _NC_CACHE:
        _NC_CACHE[0] = _build(dbg=0)
    res = run_bass_kernel_spmd(_NC_CACHE[0], in_maps, core_ids=list(range(8)))
    return _assemble(res.results)


# version 09
# speedup vs baseline: 1.2066x; 1.2066x over previous
import numpy as np
import ml_dtypes
import concourse.bass as bass
import concourse.mybir as mybir
from concourse.bass_utils import run_bass_kernel_spmd
from contextlib import ExitStack

F32 = mybir.dt.float32
BF16 = mybir.dt.bfloat16
I32 = mybir.dt.int32
AF = mybir.ActivationFunctionType
ALU = mybir.AluOpType
AX = mybir.AxisListType

T = 8192
D = 4096
DC = 2048
NH = 8
HD = 256
TT = 512
NT = T // TT
QB = 512
NQB = 8
NFC = 32
NOC = DC // 128
EPS = 1e-5
NPOLL = 4000000

ENGS = ("sync", "tensor", "vector", "scalar", "gpsimd")


class Sched:
    def __init__(self):
        self.ops = {e: [] for e in ENGS}
        self.cnt = {}

    def inc(self, name, amt=1):
        self.cnt[name] = self.cnt.get(name, 0) + amt
        return self.cnt[name]

    def val(self, name):
        return self.cnt.get(name, 0)

    def on(self, eng, fn, waits=(), inc_sem=None, amt=1):
        w = [x for x in waits if x is not None and x[1] > 0]
        mark = None
        if inc_sem is not None:
            mark = (inc_sem, self.inc(inc_sem, amt))
        self.ops[eng].append((fn, w, (inc_sem, amt) if inc_sem else None))
        return mark

    def dma(self, eng, sem, out, in_, waits=()):
        o, i = out, in_
        return self.on(eng, lambda e, c: e.dma_start(out=o, in_=i),
                       waits, sem, 16)

    def fence(self, sem):
        return (sem, self.val(sem))


class Roll:

    def __init__(self, n=2):
        self.m = {i: None for i in range(n)}

    def wait(self, slot):
        return self.m[slot]

    def used(self, slot, mark):
        self.m[slot] = mark


def _emit(nc, sch, semh, ectx):
    with ExitStack() as bctx:
        block = bctx.enter_context(nc.Block())

        def mk(ename):
            def run(eng):
                ctx = ectx[ename]
                ctx["eng"] = eng
                for fn, waits, inc in sch.ops[ename]:
                    for (n, v) in waits:
                        eng.wait_ge(semh[n], v)
                    ins = fn(eng, ctx)
                    if inc is not None:
                        if ins is None:
                            eng.nop().then_inc(semh[inc[0]], inc[1])
                        else:
                            ins.then_inc(semh[inc[0]], inc[1])
            return run

        @block.sync
        def _(e):
            mk("sync")(e)

        @block.tensor
        def _(e):
            mk("tensor")(e)

        @block.vector
        def _(e):
            mk("vector")(e)

        @block.scalar
        def _(e):
            mk("scalar")(e)

        @block.gpsimd
        def _(e):
            mk("gpsimd")(e)


def _build(dbg=0):
    nc = bass.Bass("TRN2", target_bir_lowering=False)

    x_fm = nc.dram_tensor("x_fm", [NFC, 128, T], F32, kind="ExternalInput").ap()
    wq_t = nc.dram_tensor("wq_t", [NOC, 128, D], F32, kind="ExternalInput").ap()
    wk_t = nc.dram_tensor("wk_t", [NOC, 128, D], F32, kind="ExternalInput").ap()
    wv_t = nc.dram_tensor("wv_t", [NOC, 128, D], F32, kind="ExternalInput").ap()
    wo_t = nc.dram_tensor("wo_t", [NFC, 128, D], F32, kind="ExternalInput").ap()
    a_st = nc.dram_tensor("a_st", [NFC, 128, 68], F32, kind="ExternalInput").ap()
    b_pk = nc.dram_tensor("b_pk", [64, D], F32, kind="ExternalInput").ap()
    e_tab = nc.dram_tensor("e_tab", [4, 64], F32, kind="ExternalInput").ap()
    pi_tab = nc.dram_tensor("pi_tab", [64, 64], F32, kind="ExternalInput").ap()
    rotck = nc.dram_tensor("rotck", [64, T], F32, kind="ExternalInput").ap()
    rotsk = nc.dram_tensor("rotsk", [64, T], F32, kind="ExternalInput").ap()
    rotcq = nc.dram_tensor("rotcq", [64, T // 2], F32, kind="ExternalInput").ap()
    rotsq = nc.dram_tensor("rotsq", [64, T // 2], F32, kind="ExternalInput").ap()
    maskb = nc.dram_tensor("maskb", [8, 128, QB], F32, kind="ExternalInput").ap()
    identf = nc.dram_tensor("identf", [128, 128], F32, kind="ExternalInput").ap()
    offs = nc.dram_tensor("offs", [1, 16], I32, kind="ExternalInput").ap()

    dbg_kind = "ExternalOutput" if dbg == 1 else "Internal"
    u_my = nc.dram_tensor("u_my", [NFC, 128, NQB * QB], BF16, kind=dbg_kind)
    k_fm = nc.dram_tensor("k_fm", [NH, 2, 128, T], BF16, kind=dbg_kind)
    v_tm = nc.dram_tensor("v_tm", [NH, T, HD], BF16, kind=dbg_kind)
    q_my = nc.dram_tensor("q_my", [NH, 2, 128, NQB * QB], BF16, kind=dbg_kind)
    nrm_dbg = nc.dram_tensor("nrm_dbg", [2, NH], F32, kind=dbg_kind)
    sh_O = nc.dram_tensor("sh_O", [D, NQB * QB], BF16, kind="Internal",
                          addr_space="Shared")
    sh_flag = nc.dram_tensor("sh_flag", [2, 16], I32, kind="Internal",
                             addr_space="Shared")
    o_dbg = nc.dram_tensor("o_dbg", [D, NQB * QB], BF16,
                           kind="ExternalOutput" if dbg == 2 else "Internal")
    vbf_h = nc.dram_tensor("vbf_h", [NFC, 128, 4 * QB], BF16, kind="Internal")
    w_bf = nc.dram_tensor("w_bf", [3, NOC, 128, D], BF16, kind="Internal")
    o_loc = nc.dram_tensor("o_loc", [DC, NQB * QB], BF16, kind="Internal")
    u_sel2 = nc.dram_tensor("u_sel2", [NFC, 128, 4 * QB], BF16,
                            kind="Internal")
    v_my = nc.dram_tensor("v_my", [NFC, 128, 4 * QB], F32, kind="Internal")
    out_my = nc.dram_tensor("out_my", [NFC, 128, 4 * QB], F32,
                            kind="ExternalOutput")

    sch = Sched()
    ectx = {e: {} for e in ENGS}
    WT = {"k": wk_t, "v": wv_t, "q": wq_t}

    with ExitStack() as top:
        E = top.enter_context

        ast_sb = E(nc.sbuf_tensor("ast_sb", [128, NFC * 68], BF16))
        bpk_sb = E(nc.sbuf_tensor("bpk_sb", [64, D], BF16))
        etab_sb = E(nc.sbuf_tensor("etab_sb", [4, 64], BF16))
        pi_sb = E(nc.sbuf_tensor("pi_sb", [64, 64], F32))
        mask_sb = E(nc.sbuf_tensor("mask_sb", [128, 8 * QB], BF16))
        ident_sb = E(nc.sbuf_tensor("ident_sb", [128, 128], BF16))
        ones_bf = E(nc.sbuf_tensor("ones_bf", [128, 1], BF16))
        ones_f = E(nc.sbuf_tensor("ones_f", [1, 128], F32))
        offs_sb = E(nc.sbuf_tensor("offs_sb", [1, 16], I32))
        csA_b = E(nc.sbuf_tensor("csA_b", [1, 68], BF16))
        bias_sb = E(nc.sbuf_tensor("bias_sb", [128, NH], F32))
        flagw_sb = E(nc.sbuf_tensor("flagw_sb", [1, 16], I32))
        flagr_sb = E(nc.sbuf_tensor("flagr_sb", [1, 16], I32))

        semh = {}
        for s in ("ds", "dg", "da", "tE", "vE", "aE", "gE", "pp"):
            semh[s] = E(nc.semaphore(s))

        m_cons = [
            sch.dma("gpsimd", "dg", ast_sb.rearrange("p (c k) -> p c k", k=68),
                    a_st.rearrange("c p k -> p c k")),
            sch.dma("gpsimd", "dg", bpk_sb[:, :], b_pk),
            sch.dma("gpsimd", "dg", etab_sb[:, :], e_tab),
            sch.dma("gpsimd", "dg", mask_sb.rearrange("p (s c) -> p s c",
                                                      c=QB),
                    maskb.rearrange("s p c -> p s c")),
            sch.dma("gpsimd", "dg", ident_sb[:, :], identf),
            sch.dma("sync", "ds", pi_sb[:, :], pi_tab),
            sch.dma("sync", "ds", offs_sb[:, :], offs),
        ]

        def _init(e, c):
            e.memset(ones_bf[:, :], 1.0)
            e.memset(ones_f[:, :], 1.0)
            return e.memset(bias_sb[:, :], 0.0)
        m_init = sch.on("vector", _init, m_cons, "vE")

        fe = ExitStack()
        Ef = fe.enter_context
        csW_b = Ef(nc.sbuf_tensor("csW_b", [1, 3 * NOC * 128], BF16))
        xb = Ef(nc.sbuf_tensor("xb", [128, NFC * TT], BF16))
        xf = [Ef(nc.sbuf_tensor(f"xf{i}", [128, TT], F32)) for i in range(2)]
        sqb = [Ef(nc.sbuf_tensor(f"sqb{i}", [128, TT], BF16)) for i in range(2)]
        wsb = [Ef(nc.sbuf_tensor(f"wsb{i}", [128, D], BF16)) for i in range(2)]
        mid_f = Ef(nc.sbuf_tensor("mid_f", [68, TT], F32))
        e4 = Ef(nc.sbuf_tensor("e4", [4, TT], F32))
        g4_b = Ef(nc.sbuf_tensor("g4_b", [4, TT], BF16))
        gmid_b = Ef(nc.sbuf_tensor("gmid_b", [64, TT], BF16))
        r1bc = Ef(nc.sbuf_tensor("r1bc", [68, TT], F32))
        r2bc = Ef(nc.sbuf_tensor("r2bc", [128, TT], F32))
        RK = ("mean", "var", "rstd", "nmean", "tmp")
        srow = {k: Ef(nc.sbuf_tensor(f"s_{k}", [1, TT], F32)) for k in RK}
        nmean_b = Ef(nc.sbuf_tensor("nmean_b", [1, TT], BF16))
        GK = ("r0", "r1", "r2", "r3", "w0", "w1", "M", "T2", "den", "rec")
        grow = {k: Ef(nc.sbuf_tensor(f"g_{k}", [1, TT], F32)) for k in GK}
        erow = [Ef(nc.sbuf_tensor(f"e_{i}", [1, TT], F32)) for i in range(4)]
        kvq_b = [Ef(nc.sbuf_tensor(f"kvq{i}", [128, TT], BF16))
                 for i in range(2)]
        ksc = Ef(nc.sbuf_tensor("ksc", [64, TT], F32))
        kb1 = Ef(nc.sbuf_tensor("kb1", [64, TT], F32))
        rotc_sb = Ef(nc.sbuf_tensor("rotc_sb", [64, TT], F32))
        rots_sb = Ef(nc.sbuf_tensor("rots_sb", [64, TT], F32))
        rotcq_sb = Ef(nc.sbuf_tensor("rotcq_sb", [64, TT], F32))
        rotsq_sb = Ef(nc.sbuf_tensor("rotsq_sb", [64, TT], F32))
        vrow = Ef(nc.sbuf_tensor("vrow", [128, 4 * DC], BF16))
        mean1_sb = Ef(nc.sbuf_tensor("mean1_sb", [1, TT], F32))
        csB_f = Ef(nc.sbuf_tensor("csB_f", [64, 1], F32))
        csB_bb = Ef(nc.sbuf_tensor("csB_bb", [64, 1], BF16))

        ps_stA = Ef(nc.psum_tensor("ps_stA", [1, TT], F32))
        ps_stB = Ef(nc.psum_tensor("ps_stB", [1, TT], F32))
        ps_mid = Ef(nc.psum_tensor("ps_mid", [68, TT], F32))
        ps_bc = Ef(nc.psum_tensor("ps_bc", [128, TT], F32))
        ps_o = [Ef(nc.psum_tensor(f"ps_o{i}", [128, TT], F32))
                for i in range(2)]
        ps_aux = Ef(nc.psum_tensor("ps_aux", [64, TT], F32))
        ps_vtb = Ef(nc.psum_tensor("ps_vtb", [128, 4 * 128], BF16))

        r_xf = Roll()
        r_sqb = Roll()
        r_wsb = Roll()
        r_kvq = Roll()
        r_pso = Roll()
        r_stat = Roll(1)
        r_mid = Roll(1)
        r_bc = Roll(1)
        r_aux = Roll(1)
        r_nrmrow = Roll(1)
        r_vtb = Roll(1)
        r_vrow = Roll(1)
        r_gate = {}

        def gw(name):
            return r_gate.get(name)

        def gs(name, mark):
            r_gate[name] = mark

        def mm_csa(e, c):
            last = None
            for fc in range(NFC):
                last = e.matmul(ps_aux[0:1, 0:68], ones_bf[:, 0:1],
                                ast_sb[:, fc * 68:(fc + 1) * 68],
                                start=(fc == 0), stop=(fc == NFC - 1))
            return last
        m = sch.on("tensor", mm_csa, m_cons + [m_init], "tE")
        m_csA = sch.on("vector",
                       lambda e, c: e.tensor_copy(csA_b[:, :],
                                                  ps_aux[0:1, 0:68]),
                       [m], "vE")
        r_nrmrow.used(0, m_csA)

        def _csb(e, c):
            e.reduce_sum(csB_f[:, :], bpk_sb[:, :], axis=AX.X)
            return e.tensor_copy(csB_bb[:, :], csB_f[:, :])
        m_csB = sch.on("vector", _csb, [m_cons[1]], "vE")
        r_wst = Roll()
        cache_mark = {}
        cs_prev = [None]

        def ln_stats(wait_mark, mode=1):
            def f2(e, c, mode=mode):
                if mode == 1:
                    e.tensor_scalar_mul(srow["mean"][:, :], ps_stA[:, :],
                                        1.0 / D)
                    e.tensor_copy(mean1_sb[:, :], srow["mean"][:, :])
                else:
                    e.tensor_scalar_mul(srow["mean"][:, :], ps_stA[:, :],
                                        1.0 / D)
                    e.tensor_add(srow["mean"][:, :], srow["mean"][:, :],
                                 mean1_sb[:, :])
                e.tensor_scalar_mul(srow["tmp"][:, :], ps_stB[:, :], 1.0 / D)
                e.tensor_mul(srow["var"][:, :], srow["mean"][:, :],
                             srow["mean"][:, :])
                e.tensor_sub(srow["var"][:, :], srow["tmp"][:, :],
                             srow["var"][:, :])
                e.tensor_scalar_add(srow["var"][:, :], srow["var"][:, :], EPS)
                e.tensor_scalar_mul(srow["nmean"][:, :], srow["mean"][:, :],
                                    -1.0)
                return e.tensor_copy(nmean_b[:, :], srow["nmean"][:, :])
            mv = sch.on("vector", f2, [wait_mark], "vE")

            def f2r(e, c):
                e.activation(srow["tmp"][:, :], srow["var"][:, :], AF.Ln)
                return e.activation(srow["rstd"][:, :], srow["tmp"][:, :],
                                    AF.Exp, scale=-0.5)
            ma = sch.on("scalar", f2r, [mv], "aE")
            return mv, ma

        for ti in range(NT):
            even = (ti % 2 == 0)
            tc = slice(ti * TT, (ti + 1) * TT)

            for fc in range(NFC):
                buf = fc % 2
                w = []
                if fc == 0:
                    w += [r_stat.wait(0), r_mid.wait(0), gw("xbuse"),
                          gw("ulast")]
                mld = sch.dma("gpsimd", "dg", xb[:, fc * TT:(fc + 1) * TT],
                              x_fm[fc, :, tc], waits=w)
                m_sq = sch.on(
                    "scalar",
                    lambda e, c, fc=fc, buf=buf: e.activation(
                        sqb[buf][:, :], xb[:, fc * TT:(fc + 1) * TT],
                        AF.Square),
                    [mld, r_sqb.wait(buf)], "aE")

                def mm_f1(e, c, fc=fc, buf=buf):
                    xs = xb[:, fc * TT:(fc + 1) * TT]
                    e.matmul(ps_stA[:, :], ones_bf[:, 0:1], xs,
                             start=(fc == 0), stop=(fc == NFC - 1))
                    e.matmul(ps_mid[:, :], ast_sb[:, fc * 68:(fc + 1) * 68],
                             xs, start=(fc == 0), stop=False)
                    return e.matmul(ps_stB[:, :], ones_bf[:, 0:1],
                                    sqb[buf][:, :], start=(fc == 0),
                                    stop=(fc == NFC - 1))
                wmm = [mld, m_sq]
                m_f1 = sch.on("tensor", mm_f1, wmm, "tE")
                r_sqb.used(buf, m_f1)

            m_ln1v, m_r1 = ln_stats(m_f1)
            r_stat.used(0, m_ln1v)

            def f3mm(e, c):
                e.matmul(ps_mid[:, :], csA_b[:, :], nmean_b[:, :],
                         start=False, stop=True)
                return e.matmul(ps_bc[0:68, :], ones_f[:, 0:68],
                                srow["rstd"][:, :], start=True, stop=True)
            m_f3 = sch.on("tensor", f3mm, [m_ln1v, m_r1, r_bc.wait(0)], "tE")

            def f3ev(e, c):
                e.tensor_copy(r1bc[:, :], ps_bc[0:68, :])
                return e.tensor_mul(mid_f[:, :], ps_mid[:, :], r1bc[:, :])
            m_midev = sch.on("vector", f3ev, [m_f3, gw("midfree")], "vE")
            r_mid.used(0, m_midev)
            r_bc.used(0, m_midev)

            w4 = [m_midev, gw("growfree")]
            for i2 in range(4):
                m_gr = sch.dma("gpsimd", "dg", grow[f"r{i2}"][:, :],
                               mid_f[64 + i2:65 + i2, :],
                               waits=w4 if i2 == 0 else ())
            m_e4 = sch.on("scalar",
                          lambda e, c: e.activation(e4[:, :],
                                                    mid_f[64:68, :], AF.Exp),
                          [m_midev, gw("e4free")], "aE")
            for i2 in range(4):
                m_ge = sch.dma("gpsimd", "dg", erow[i2][:, :],
                               e4[i2:i2 + 1, :],
                               waits=[m_e4, gw("erowfree")]
                               if i2 == 0 else ())

            def f4gate(e, c):
                g = grow
                e.tensor_max(g["w0"][:, :], g["r0"][:, :], g["r1"][:, :])
                e.tensor_tensor(g["M"][:, :], g["r0"][:, :], g["r1"][:, :],
                                ALU.min)
                e.tensor_max(g["w1"][:, :], g["r2"][:, :], g["r3"][:, :])
                e.tensor_tensor(g["T2"][:, :], g["r2"][:, :], g["r3"][:, :],
                                ALU.min)
                e.tensor_max(g["den"][:, :], g["M"][:, :], g["T2"][:, :])
                e.tensor_tensor(g["T2"][:, :], g["w0"][:, :], g["w1"][:, :],
                                ALU.min)
                e.tensor_max(g["T2"][:, :], g["T2"][:, :], g["den"][:, :])
                for i2 in range(4):
                    e.tensor_tensor(g[f"r{i2}"][:, :], g[f"r{i2}"][:, :],
                                    g["T2"][:, :], ALU.is_ge)
                for i2 in range(4):
                    e.tensor_mul(erow[i2][:, :], erow[i2][:, :],
                                 g[f"r{i2}"][:, :])
                e.tensor_add(g["den"][:, :], erow[0][:, :], erow[1][:, :])
                e.tensor_add(g["den"][:, :], g["den"][:, :], erow[2][:, :])
                e.tensor_add(g["den"][:, :], g["den"][:, :], erow[3][:, :])
                e.reciprocal(g["rec"][:, :], g["den"][:, :])
                last = None
                for i2 in range(4):
                    last = e.tensor_mul(erow[i2][:, :], erow[i2][:, :],
                                        g["rec"][:, :])
                return last
            m_gate = sch.on("vector", f4gate, [m_gr, m_ge], "vE")
            gs("growfree", m_gate)
            gs("e4free", m_ge)
            for i2 in range(4):
                m_g4 = sch.dma("gpsimd", "dg", g4_b[i2:i2 + 1, :],
                               erow[i2][:, :],
                               waits=[m_gate, gw("g4free")] if i2 == 0 else ())
            gs("erowfree", m_g4)

            def f4g(e, c):
                return e.matmul(ps_bc[0:64, :], etab_sb[:, :], g4_b[:, :],
                                start=True, stop=True)
            m_gmm = sch.on("tensor", f4g, [m_g4, r_bc.wait(0)], "tE")
            gs("g4free", m_gmm)
            m_gmid = sch.on(
                "vector",
                lambda e, c: e.tensor_mul(gmid_b[:, :], ps_bc[0:64, :],
                                          mid_f[0:64, :]),
                [m_gmm, gw("gmidfree")], "vE")
            gs("midfree", m_gmid)
            r_bc.used(0, m_gmid)

            m_csbg = sch.on(
                "tensor",
                lambda e, c: e.matmul(ps_stA[:, :], csB_bb[:, 0:1],
                                      gmid_b[:, :], start=True, stop=True),
                [m_gmid, r_stat.wait(0), gw("csbg"), m_csB], "tE")

            def emit_stb(args):
                oc2, buf2, musq = args
                def mm_f5(e, c, oc=oc2, buf=buf2):
                    return e.matmul(ps_stB[:, :], ones_bf[:, 0:1],
                                    sqb[buf][:, :], start=(oc == 0),
                                    stop=(oc == NFC - 1))
                w5 = [musq]
                if oc2 == 0:
                    w5.append(r_stat.wait(0))
                m5 = sch.on("tensor", mm_f5, w5, "tE")
                r_sqb.used(buf2, m5)
                return m5
            pend_stb = None
            for oc in range(NFC):
                buf = oc % 2
                m_x2 = sch.dma("scalar", "da", xf[buf][:, :], x_fm[oc, :, tc],
                               waits=[r_xf.wait(buf)])

                def mm_d(e, c, oc=oc, buf=buf):
                    return e.matmul(ps_o[buf][:, :],
                                    bpk_sb[:, oc * 128:(oc + 1) * 128],
                                    gmid_b[:, :], start=True, stop=True)
                m_dmm = sch.on("tensor", mm_d, [m_gmid, r_pso.wait(buf)],
                               "tE")
                if pend_stb is not None:
                    emit_stb(pend_stb)

                def uadd(e, c, oc=oc, buf=buf):
                    return e.tensor_add(xb[:, oc * TT:(oc + 1) * TT],
                                        ps_o[buf][:, :], xf[buf][:, :])
                m_ua = sch.on("vector", uadd, [m_dmm, m_x2], "vE")
                r_xf.used(buf, m_ua)
                r_pso.used(buf, m_ua)
                if even:
                    ug = ti // 2
                    m_ust = sch.dma(
                        "scalar", "da",
                        u_my.ap()[oc, :, ug * QB:(ug + 1) * QB],
                        xb[:, oc * TT:(oc + 1) * TT], waits=[m_ua])
                m_usq = sch.on(
                    "scalar",
                    lambda e, c, oc=oc, buf=buf: e.activation(
                        sqb[buf][:, :], xb[:, oc * TT:(oc + 1) * TT],
                        AF.Square),
                    [m_ua, r_sqb.wait(buf)], "aE")
                pend_stb = (oc, buf, m_usq)
            m_f5 = emit_stb(pend_stb)
            gs("csbg", m_f5)
            if even:
                gs("ulast", m_ust)

            m_ln2v, m_r2 = ln_stats(m_f5, mode=2)
            r_stat.used(0, m_ln2v)

            def f6bc(e, c):
                return e.matmul(ps_bc[:, :], ones_f[:, 0:128],
                                srow["rstd"][:, :], start=True, stop=True)
            m_f6 = sch.on("tensor", f6bc, [m_ln2v, m_r2, r_bc.wait(0)], "tE")
            m_r2ev = sch.on("vector",
                            lambda e, c: e.tensor_copy(r2bc[:, :],
                                                       ps_bc[:, :]),
                            [m_f6, gw("r2free")], "vE")
            r_bc.used(0, m_r2ev)

            m_rc = sch.dma("sync", "ds", rotc_sb[:, :], rotck[:, tc],
                           waits=[gw("rotuse")])
            m_rot = sch.dma("sync", "ds", rots_sb[:, :], rotsk[:, tc])
            if even:
                g2 = ti // 2
                m_rcq = sch.dma("sync", "ds", rotcq_sb[:, :],
                                rotcq[:, g2 * QB:(g2 + 1) * QB],
                                waits=[gw("rotuseq")])
                m_rotq = sch.dma("sync", "ds", rotsq_sb[:, :],
                                 rotsq[:, g2 * QB:(g2 + 1) * QB])

            pend_rot = None
            pend_v = None
            for w_i, wn in enumerate(("k", "v", "q")):
                if wn == "q" and not even:
                    continue
                for oc in range(NOC):
                    buf = oc % 2
                    h, dcx = oc // 2, oc % 2
                    col = (w_i * NOC + oc) * 128
                    if ti == 0:
                        wld = sch.dma(
                            "gpsimd", "dg", wsb[buf][:, :],
                            WT[wn][oc, :, :],
                            waits=[r_wsb.wait(buf), r_wst.wait(buf)])

                        def mm_cs(e, c, buf=buf):
                            last = None
                            for fc in range(NFC):
                                last = e.matmul(
                                    ps_bc[0:1, 0:128], ones_bf[:, 0:1],
                                    wsb[buf][:, fc * 128:(fc + 1) * 128],
                                    start=(fc == 0), stop=(fc == NFC - 1))
                            return last
                        m_cmm = sch.on("tensor", mm_cs,
                                       [wld, cs_prev[0], m_r2ev], "tE")
                        m_cse = sch.on(
                            "vector",
                            lambda e, c, col=col: e.tensor_copy(
                                csW_b[:, col:col + 128], ps_bc[0:1, 0:128]),
                            [m_cmm], "vE")
                        cs_prev[0] = m_cse
                        m_wst = sch.dma("sync", "ds", w_bf.ap()[w_i, oc, :, :],
                                        wsb[buf][:, :], waits=[wld])
                        cache_mark[(w_i, oc)] = m_wst
                        r_wst.used(buf, m_wst)
                        wex = [m_cse]
                    else:
                        w0 = [r_wsb.wait(buf), r_wst.wait(buf)]
                        first_reuse = (ti == 1) or (wn == "q" and ti == 2)
                        if first_reuse:
                            w0.append(cache_mark[(w_i, oc)])
                        qn, qs = (("gpsimd", "dg") if oc % 2 == 0
                                  else ("sync", "ds"))
                        wld = sch.dma(
                            qn, qs, wsb[buf][:, :],
                            w_bf.ap()[w_i, oc, :, :], waits=w0)
                        wex = []

                    def mm_p(e, c, buf=buf, oc=oc, col=col):
                        pso = ps_o[oc % 2]
                        for fc in range(NFC):
                            e.matmul(pso[:, :],
                                     wsb[buf][:, fc * 128:(fc + 1) * 128],
                                     xb[:, fc * TT:(fc + 1) * TT],
                                     start=(fc == 0), stop=False)
                        return e.matmul(pso[:, :], csW_b[:, col:col + 128],
                                        nmean_b[:, :], start=False, stop=True)
                    m_pmm = sch.on("tensor", mm_p,
                                   [wld, m_r2ev, r_pso.wait(oc % 2)] + wex,
                                   "tE")
                    r_wsb.used(buf, m_pmm)

                    if pend_rot is not None:
                        ph, pkv, m_evp = pend_rot
                        pend_rot = None

                        def mm_swp(e, c):
                            return e.matmul(ps_aux[:, :], pi_sb[:, :],
                                            ksc[:, :], start=True, stop=True)
                        m_swp = sch.on("tensor", mm_swp,
                                       [m_evp, r_aux.wait(0)], "tE")
                        cs_c = rotc_sb if wn == "k" else rotcq_sb
                        cs_s = rots_sb if wn == "k" else rotsq_sb
                        m_rw = m_rot if wn == "k" else m_rotq

                        def blend(e, c, pkv=pkv, cs_c=cs_c, cs_s=cs_s):
                            e.tensor_mul(kb1[:, :], ps_aux[:, :], cs_s[:, :])
                            e.tensor_mul(ksc[:, :], ksc[:, :], cs_c[:, :])
                            return e.tensor_add(pkv[0:64, :], kb1[:, :],
                                                ksc[:, :])
                        m_bl = sch.on("vector", blend, [m_swp, m_rw], "vE")
                        gs("kscfree", m_bl)
                        r_aux.used(0, m_bl)
                        if wn == "k":
                            m_st = sch.dma("sync", "ds",
                                           k_fm.ap()[ph, 0, :, tc],
                                           pkv[:, :], waits=[m_bl])
                        else:
                            g2 = ti // 2
                            m_st = sch.dma("sync", "ds",
                                           q_my.ap()[ph, 0, :,
                                                     g2 * QB:(g2 + 1) * QB],
                                           pkv[:, :], waits=[m_bl])
                        r_kvq.used(0, m_st)
                    if pend_v is not None:
                        poc, pkv, m_pev = pend_v
                        pend_v = None

                        def vtr(e, c, pkv=pkv):
                            last = None
                            for rb in range(4):
                                last = e.transpose(
                                    ps_vtb[:, rb * 128:(rb + 1) * 128],
                                    pkv[:, rb * 128:(rb + 1) * 128],
                                    ident_sb[:, :])
                            return last
                        m_tr = sch.on("tensor", vtr,
                                      [m_pev, r_vtb.wait(0)], "tE")
                        r_kvq.used(poc % 2, m_tr)

                        def vev(e, c, poc=poc):
                            dst = vrow.rearrange("p (rb cc) -> p rb cc",
                                                 cc=DC)
                            return e.tensor_copy(
                                dst[:, :, poc * 128:(poc + 1) * 128],
                                ps_vtb.rearrange("p (rb cc) -> p rb cc",
                                                 cc=128))
                        m_vev = sch.on("vector", vev,
                                       [m_tr, r_vrow.wait(0)
                                        if poc == 0 else None], "vE")
                        r_vtb.used(0, m_vev)

                    kv = kvq_b[oc % 2]
                    is_rot = (wn in ("k", "q")) and (dcx == 0)
                    wkv = [r_kvq.wait(oc % 2), m_pmm]
                    if is_rot:
                        def ev2(e, c, oc=oc, kv=kv):
                            e.tensor_mul(ksc[:, :], ps_o[oc % 2][0:64, :],
                                         r2bc[0:64, :])
                            return e.tensor_mul(kv[64:128, :],
                                                ps_o[oc % 2][64:128, :],
                                                r2bc[64:128, :])
                        m_evp = sch.on("vector", ev2,
                                       wkv + [gw("kscfree")], "vE")
                        r_pso.used(oc % 2, m_evp)
                        pend_rot = (h, kv, m_evp)
                    else:
                        m_ev = sch.on(
                            "vector",
                            lambda e, c, oc=oc, kv=kv: e.tensor_mul(
                                kv[:, :], ps_o[oc % 2][:, :], r2bc[:, :]),
                            wkv, "vE")
                        r_pso.used(oc % 2, m_ev)
                        if wn == "v":
                            pend_v = (oc, kv, m_ev)
                        elif wn == "k":
                            m_st = sch.dma("sync", "ds",
                                           k_fm.ap()[h, 1, :, tc],
                                           kv[:, :], waits=[m_ev])
                            r_kvq.used(oc % 2, m_st)
                        else:
                            g2 = ti // 2
                            m_st = sch.dma("sync", "ds",
                                           q_my.ap()[h, 1, :,
                                                     g2 * QB:(g2 + 1) * QB],
                                           kv[:, :], waits=[m_ev])
                            r_kvq.used(oc % 2, m_st)
                if wn == "v":
                    poc, pkv, m_pev = pend_v
                    pend_v = None

                    def vtr_l(e, c, pkv=pkv):
                        last = None
                        for rb in range(4):
                            last = e.transpose(
                                ps_vtb[:, rb * 128:(rb + 1) * 128],
                                pkv[:, rb * 128:(rb + 1) * 128],
                                ident_sb[:, :])
                        return last
                    m_tr = sch.on("tensor", vtr_l, [m_pev, r_vtb.wait(0)],
                                  "tE")
                    r_kvq.used(poc % 2, m_tr)

                    def vev_l(e, c, poc=poc):
                        dst = vrow.rearrange("p (rb cc) -> p rb cc", cc=DC)
                        return e.tensor_copy(
                            dst[:, :, poc * 128:(poc + 1) * 128],
                            ps_vtb.rearrange("p (rb cc) -> p rb cc", cc=128))
                    m_vev = sch.on("vector", vev_l, [m_tr], "vE")
                    r_vtb.used(0, m_vev)
                    src4 = vrow.rearrange("p (rb hh d) -> p rb hh d",
                                          rb=4, hh=NH)
                    for h2 in range(NH):
                        m_vst = sch.dma(
                            "sync", "ds",
                            v_tm.ap()[h2, ti * TT:(ti + 1) * TT, :]
                            .rearrange("(rb p) d -> p rb d", p=128),
                            src4[:, :, h2, :],
                            waits=[m_vev] if h2 == 0 else ())
                    r_vrow.used(0, m_vst)
            if ti == 0:
                r_bc.used(0, cs_prev[0])
            gs("xbuse", m_pmm)
            gs("r2free", m_pmm)
            gs("gmidfree", ("vE", sch.val("vE")))
            gs("rotuse", ("vE", sch.val("vE")))
            if even:
                gs("rotuseq", ("vE", sch.val("vE")))

        fe.close()

        m_bias = m_init

        def g_regs(e, c):
            c["p16"] = e.alloc_register("p16")
            e.reg_load(c["p16"], offs_sb[0:1, 0:1])
            c["p16v"] = e.snap(e.compute_val(c["p16"]), min_val=0, max_val=16)
            c["parv"] = e.snap(c["p16v"] // 16, min_val=0, max_val=1)
            for bb2 in range(4):
                r = e.alloc_register(f"uc{bb2}")
                e.reg_load(r, offs_sb[0:1, 1 + bb2:2 + bb2])
                c[f"ucv{bb2}"] = e.snap(e.compute_val(r), min_val=0,
                                        max_val=7)
            return None
        sch.on("gpsimd", g_regs, [m_cons[6]])

        m_ust_cp = []
        for bb in range(4):
            def u_cp(e, c, bb=bb):
                return e.dma_start(
                    out=u_sel2.ap()[:, :, bb * QB:(bb + 1) * QB],
                    in_=u_my.ap()[:, :, bass.ts(c[f"ucv{bb}"], QB)])
            m_ust_cp.append(sch.on(
                "gpsimd", u_cp,
                [("da", sch.val("da"))] if bb == 0 else (), "gE", 16))
        m_ustage = m_ust_cp[-1]

        if dbg == 1:
            sch.dma("sync", "ds", nrm_dbg.ap()[0:1, 0:NH],
                    bias_sb[0:1, 0:NH],
                    waits=[m_bias, ("da", sch.val("da")),
                           ("dg", sch.val("dg"))])
            m_fin = sch.dma("sync", "ds", nrm_dbg.ap()[1:2, 0:NH],
                            bias_sb[0:1, 0:NH])
            sch.on("sync", lambda e, c: None, [("ds", sch.val("ds"))])
            _emit(nc, sch, semh, ectx)
            return nc

        at = ExitStack()
        Ea = at.enter_context
        ksl = [Ea(nc.sbuf_tensor(f"ksl{i}", [128, 2 * T], BF16))
               for i in range(2)]
        vsl = [Ea(nc.sbuf_tensor(f"vsl{i}", [128, 64 * HD], BF16))
               for i in range(2)]
        qsl = [Ea(nc.sbuf_tensor(f"qsl{i}", [128, 2 * NQB * QB], BF16))
               for i in range(2)]
        p_sb = [[Ea(nc.sbuf_tensor(f"p_sb{a}{b}", [128, QB], BF16))
                 for b in range(2)] for a in range(2)]
        lrow = [Ea(nc.sbuf_tensor(f"lrow{a}", [1, QB], F32)) for a in range(2)]
        rrow = [Ea(nc.sbuf_tensor(f"rrow{a}", [1, QB], F32)) for a in range(2)]
        recbc = Ea(nc.sbuf_tensor("recbc", [128, QB], F32))
        obf = Ea(nc.sbuf_tensor("obf", [128, QB], BF16))

        ps_sc = [Ea(nc.psum_tensor(f"ps_sc{a}", [128, QB], F32))
                 for a in range(2)]
        ps_O = [[Ea(nc.psum_tensor(f"ps_O{a}{d2}", [128, QB], F32))
                 for d2 in range(2)] for a in range(2)]
        ps_rs = [Ea(nc.psum_tensor(f"ps_rs{a}", [1, QB], F32))
                 for a in range(2)]

        r_ksl = Roll()
        r_vsl = Roll()
        r_qsl = Roll()
        r_pbuf = {}
        g_ost = {}

        for h in range(NH):
            hb = h % 2
            m_kl = sch.dma("sync", "ds",
                           ksl[hb].rearrange("p (d t) -> p d t", t=T),
                           k_fm.ap()[h, :, :, :].rearrange("d p t -> p d t"),
                           waits=[r_ksl.wait(hb), ("vE", sch.val("vE"))
                                  if h == 0 else None])
            m_vl0 = None
            for part in range(4):
                m_vl = sch.dma(
                    "sync", "ds",
                    vsl[hb].rearrange("p (kc d) -> p kc d", d=HD)
                    [:, part * 16:(part + 1) * 16, :],
                    v_tm.ap()[h, part * 2048:(part + 1) * 2048, :]
                    .rearrange("(kc p) d -> p kc d", p=128),
                    waits=[r_vsl.wait(hb)] if part == 0 else ())
            m_ql = sch.dma("sync", "ds",
                           qsl[hb].rearrange("p (d q) -> p d q", q=NQB * QB),
                           q_my.ap()[h, :, :, :].rearrange("d p q -> p d q"),
                           waits=[r_qsl.wait(hb)])

            for qbp in range(4):
                qA, qAn = 2 * qbp, 16 * qbp + 8
                qBi, qBn = 2 * qbp + 1, 16 * qbp + 16
                exp_mark = {}
                qk_mark = {}
                pfin_mark = {}
                for kc in range(qBn):
                    valid = ([(0, qA, qAn)] if kc < qAn else []) + \
                        [(1, qBi, qBn)]
                    def mm_qk(e, c, kc=kc, valid=tuple(valid), h=h, hb=hb):
                        last = None
                        for d2 in range(2):
                            for (sl, qb, _) in valid:
                                last = e.matmul(
                                    ps_sc[sl][:, :],
                                    ksl[hb][:, d2 * T + kc * 128:
                                            d2 * T + (kc + 1) * 128],
                                    qsl[hb][:, d2 * NQB * QB + qb * QB:
                                            d2 * NQB * QB + (qb + 1) * QB],
                                    start=(d2 == 0), stop=(d2 == 1))
                        return last
                    wqk = [m_kl, m_ql]
                    for (sl, qb, qn) in valid:
                        wqk.append(exp_mark.get((sl, kc - 1)))
                    if kc == 0:
                        wqk.append(gw(("scfree", 0)))
                        wqk.append(gw(("scfree", 1)))
                        wqk.append(m_bias)
                    qk_mark[kc] = sch.on("tensor", mm_qk, wqk, "tE")

                    for (sl, qb, qn) in valid:
                        pt = p_sb[sl][kc % 2]

                        def expf(e, c, sl=sl, pt=pt, h=h):
                            return e.activation(pt[:, :], ps_sc[sl][:, :],
                                                AF.Exp,
                                                bias=bias_sb[:, h:h + 1],
                                                scale=1.0 / 16.0)
                        we = [qk_mark[kc], r_pbuf.get((sl, kc % 2))]
                        exp_mark[(sl, kc)] = sch.on("scalar", expf, we, "aE")
                        pfin_mark[(sl, kc)] = exp_mark[(sl, kc)]
                        rel = kc - (qn - 8)
                        if 0 <= rel < 8:
                            def maskf(e, c, sl=sl, pt=pt, rel=rel):
                                return e.tensor_mul(
                                    pt[:, :], pt[:, :],
                                    mask_sb[:, rel * QB:(rel + 1) * QB])
                            pfin_mark[(sl, kc)] = sch.on(
                                "vector", maskf, [exp_mark[(sl, kc)]], "vE")

                    def emit_pv(kc2):
                        vald2 = ([(0, qA, qAn)] if kc2 < qAn else []) + \
                            [(1, qBi, qBn)]

                        def mm_pv(e, c, kc2=kc2, vald2=tuple(vald2), hb=hb):
                            last = None
                            for d2 in range(2):
                                for (sl, qb, qn) in vald2:
                                    last = e.matmul(
                                        ps_O[sl][d2][:, :],
                                        vsl[hb][:, kc2 * HD + d2 * 128:
                                                kc2 * HD + (d2 + 1) * 128],
                                        p_sb[sl][kc2 % 2][:, :],
                                        start=(kc2 == 0),
                                        stop=(kc2 == qn - 1))
                            for (sl, qb, qn) in vald2:
                                last = e.matmul(
                                    ps_rs[sl][:, :], ones_bf[:, 0:1],
                                    p_sb[sl][kc2 % 2][:, :],
                                    start=(kc2 == 0), stop=(kc2 == qn - 1))
                            return last
                        wpv = [m_vl, pfin_mark[(0, kc2)]
                               if kc2 < qAn else None, pfin_mark[(1, kc2)]]
                        if kc2 == 0:
                            wpv += [gw(("ofree", 0)), gw(("ofree", 1)),
                                    gw(("rsfree", 0)), gw(("rsfree", 1))]
                        pv = sch.on("tensor", mm_pv, wpv, "tE")
                        for (sl, qb, qn) in vald2:
                            r_pbuf[(sl, kc2 % 2)] = pv
                        return pv
                    if kc >= 1:
                        emit_pv(kc - 1)
                m_pvlast = emit_pv(qBn - 1)

                for sl, qb, qn in ((0, qA, qAn), (1, qBi, qBn)):
                    m_rsv = sch.on(
                        "vector",
                        lambda e, c, sl=sl: e.tensor_copy(lrow[sl][:, :],
                                                          ps_rs[sl][:, :]),
                        [m_pvlast], "vE")
                    m_rec = sch.on(
                        "vector",
                        lambda e, c, sl=sl: e.reciprocal(rrow[sl][:, :],
                                                         lrow[sl][:, :]),
                        [m_rsv], "vE")
                    gs(("rsfree", sl), m_rsv)

                    def recbcf(e, c, sl=sl):
                        return e.matmul(ps_sc[sl][:, :], ones_f[:, 0:128],
                                        rrow[sl][:, :], start=True, stop=True)
                    m_rbc = sch.on("tensor", recbcf,
                                   [m_rec, exp_mark[(sl, qn - 1)]], "tE")
                    m_rbe = sch.on(
                        "vector",
                        lambda e, c, sl=sl: e.tensor_copy(recbc[:, :],
                                                          ps_sc[sl][:, :]),
                        [m_rbc, gw("recbcuse")], "vE")
                    gs(("scfree", sl), m_rbe)
                    for d2 in range(2):
                        m_on = sch.on(
                            "vector",
                            lambda e, c, sl=sl, d2=d2: e.tensor_mul(
                                obf[:, :], ps_O[sl][d2][:, :], recbc[:, :]),
                            [m_rbe, gw("obffree")], "vE")
                        gs("recbcuse", m_on)
                        gs(("ofree", sl), m_on)

                        rr = (2 * h + d2) * 128
                        m_ost = sch.dma(
                            "scalar", "da",
                            o_loc.ap()[rr:rr + 128, qb * QB:(qb + 1) * QB],
                            obf[:, :], waits=[m_on])
                        gs("obffree", m_ost)
                        g_ost[h // 4] = m_ost
            r_ksl.used(hb, qk_mark[qBn - 1])
            r_vsl.used(hb, m_pvlast)
            r_qsl.used(hb, qk_mark[qBn - 1])

        at.close()

        m_ocp = []
        for j in range(2):
            def o_cp(e, c, j=j):
                key = f"row{j}"
                if key not in c:
                    c[key] = e.snap(c["parv"] * 2 + j, min_val=0, max_val=3)
                return e.dma_start(
                    out=sh_O.ap()[bass.ts(c[key], 8 * 128), :],
                    in_=o_loc.ap()[j * 1024:(j + 1) * 1024, :])
            m_ocp.append(sch.on("gpsimd", o_cp, [g_ost[j]], "gE", 16))
        m_obulk = ("gE", sch.val("gE"))

        def flagw(e, c):
            c["myslot"] = e.alloc_register("myslot")
            c["paslot"] = e.alloc_register("paslot")
            c["nonce"] = e.alloc_register("nonce")
            e.reg_load(c["myslot"], offs_sb[0:1, 5:6])
            e.reg_load(c["paslot"], offs_sb[0:1, 6:7])
            e.reg_load(c["nonce"], offs_sb[0:1, 7:8])
            c["mysv"] = e.snap(e.compute_val(c["myslot"]), min_val=0,
                               max_val=1)
            c["pasv"] = e.snap(e.compute_val(c["paslot"]), min_val=0,
                               max_val=1)
            return None
        sch.on("gpsimd", flagw, [m_obulk, ("ds", sch.val("ds")),
                                 ("da", sch.val("da"))])
        m_fw0 = sch.dma("gpsimd", "dg", flagw_sb[:, :], offs)
        def flag_store(e, c):
            return e.dma_start(out=sh_flag.ap()[bass.ts(c["mysv"], 1), :],
                               in_=flagw_sb[0:1, :])
        m_fst = sch.on("gpsimd", flag_store, [m_fw0], "gE", 16)

        def poll(e, c):
            with e.register("cond") as cond, e.register("tgt") as tgt, \
                    e.register("v") as v, e.register("ni") as ni, \
                    e.register("t2") as t2:
                e.reg_mov(cond, 1)
                e.reg_mov(ni, 0)
                e.reg_mov(tgt, sch.val("pp"))
                with e.While(cond):
                    e.dma_start(out=flagr_sb[:, :],
                                in_=sh_flag.ap()[bass.ts(c["pasv"], 1), :]
                                ).then_inc(semh["pp"], 16)
                    e.reg_add(tgt, tgt, 16)
                    e.wait_ge(semh["pp"], tgt)
                    e.reg_load(v, flagr_sb[0:1, 7:8])
                    e.reg_add(ni, ni, 1)
                    e.reg_sub(cond, v, c["nonce"])
                    e.reg_sub(t2, NPOLL, ni)
                    with e.If_lt(t2, 1):
                        e.reg_mov(cond, 0)
            return None
        m_poll = sch.on("gpsimd", poll, [m_fst], "gE", 1)

        if dbg == 2:
            m_cp = sch.dma("gpsimd", "dg", o_dbg.ap(), sh_O.ap(),
                           waits=[m_poll])
            sch.on("gpsimd", lambda e, c: None, [("dg", sch.val("dg"))])
            _emit(nc, sch, semh, ectx)
            return nc

        be = ExitStack()
        Eb = be.enter_context
        osb = [Eb(nc.sbuf_tensor(f"osb{b}", [128, NFC * QB], BF16))
               for b in range(4)]
        wob = [Eb(nc.sbuf_tensor(f"wob{i}", [128, D], BF16)) for i in range(2)]
        ub = [Eb(nc.sbuf_tensor(f"ub{i}", [128, 4 * QB], BF16))
              for i in range(2)]
        vf = [Eb(nc.sbuf_tensor(f"vf{i}", [128, QB], F32)) for i in range(2)]
        vb = Eb(nc.sbuf_tensor("vb", [128, QB], BF16))
        ps_vo = [Eb(nc.psum_tensor(f"ps_vo{i}", [128, QB], F32))
                 for i in range(4)]

        m_old = []
        for bb in range(4):
            def o_ld(e, c, bb=bb):
                return e.dma_start(
                    out=osb[bb].rearrange("p (fc q) -> p fc q", q=QB),
                    in_=sh_O.ap()[:, bass.ts(c[f"ucv{bb}"], QB)]
                    .rearrange("(fc p) q -> p fc q", p=128))
            m_old.append(sch.on("gpsimd", o_ld, [m_poll], "gE", 16))

        r_wob = Roll()
        r_ub = Roll()
        r_vf = Roll()
        r_vb = Roll(1)
        r_vops = [Roll(1) for _ in range(4)]

        for ec in range(NFC):
            buf = ec % 2
            wld = sch.dma("gpsimd", "dg", wob[buf][:, :],
                          wo_t[ec, :, :],
                          waits=[r_wob.wait(buf)])

            m_ul = sch.dma("scalar", "da", ub[buf][:, :],
                           u_sel2.ap()[ec, :, :],
                           waits=[r_ub.wait(buf), m_ustage])

            def mm_wo(e, c, buf=buf):
                last = None
                for dcc in range(NFC):
                    for bb in range(4):
                        last = e.matmul(
                            ps_vo[bb][:, :],
                            wob[buf][:, dcc * 128:(dcc + 1) * 128],
                            osb[bb][:, dcc * QB:(dcc + 1) * QB],
                            start=(dcc == 0),
                            stop=(dcc == NFC - 1))
                return last
            wvo = [wld] + m_old + [r_vops[bb].wait(0) for bb in range(4)]
            m_vo = sch.on("tensor", mm_wo, wvo, "tE")
            r_wob.used(buf, m_vo)

            for bb in range(4):
                def vadd(e, c, bb=bb, buf=buf):
                    return e.tensor_add(vf[buf][:, :], ps_vo[bb][:, :],
                                        ub[buf][:, bb * QB:(bb + 1) * QB])
                m_va = sch.on("vector", vadd, [m_vo, m_ul, r_vf.wait(buf)],
                              "vE")
                if bb == 3:
                    r_ub.used(buf, m_va)
                r_vops[bb].used(0, m_va)
                m_vst = sch.dma("sync", "ds",
                                v_my.ap()[ec, :, bb * QB:(bb + 1) * QB],
                                vf[buf][:, :], waits=[m_va])
                m_vbc = sch.on(
                    "vector",
                    lambda e, c, buf=buf: e.tensor_copy(vb[:, :],
                                                        vf[buf][:, :]),
                    [m_va, r_vb.wait(0)], "vE")
                r_vf.used(buf, m_vst)
                m_vbst = sch.dma("sync", "ds",
                                 vbf_h.ap()[ec, :, bb * QB:(bb + 1) * QB],
                                 vb[:, :], waits=[m_vbc])
                r_vb.used(0, m_vbst)
        m_wo_done = ("ds", sch.val("ds"))
        be.close()

        st_out = []
        for half in range(2):
            TB2 = 2 * QB
            hc = slice(half * TB2, (half + 1) * TB2)
            hs = ExitStack()
            E2 = hs.enter_context
            vbs = [E2(nc.sbuf_tensor(f"vbs{half}{i}", [128, TB2], BF16))
                   for i in range(2)]
            sq2b = [E2(nc.sbuf_tensor(f"sq2b{half}{i}", [128, TB2], BF16))
                    for i in range(2)]
            mid2 = E2(nc.sbuf_tensor(f"mid2_{half}", [68, TB2], F32))
            e42 = E2(nc.sbuf_tensor(f"e42_{half}", [4, TB2], F32))
            g42 = E2(nc.sbuf_tensor(f"g42_{half}", [4, TB2], BF16))
            gmid2 = E2(nc.sbuf_tensor(f"gmid2_{half}", [64, TB2], BF16))
            r3bc = E2(nc.sbuf_tensor(f"r3bc_{half}", [68, TB2], F32))
            srow2 = {k: E2(nc.sbuf_tensor(f"s2{half}_{k}", [1, TB2], F32))
                     for k in RK}
            nmean2 = E2(nc.sbuf_tensor(f"nmean2_{half}", [1, TB2], BF16))
            grow2 = {k: E2(nc.sbuf_tensor(f"g2{half}_{k}", [1, TB2], F32))
                     for k in GK}
            erow2 = [E2(nc.sbuf_tensor(f"e2{half}_{i}", [1, TB2], F32))
                     for i in range(4)]
            vrd = [E2(nc.sbuf_tensor(f"vrd{half}{i}", [128, QB], F32))
                   for i in range(2)]
            of = [E2(nc.sbuf_tensor(f"of{half}{i}", [128, QB], F32))
                  for i in range(2)]

            ps_b2 = E2(nc.psum_tensor(f"ps_b2_{half}", [68, TB2], F32))
            sub = ExitStack()
            ps_m2 = sub.enter_context(
                nc.psum_tensor(f"ps_m2_{half}", [68, TB2], F32))
            ps_s3A = sub.enter_context(
                nc.psum_tensor(f"ps_s3A_{half}", [1, TB2], F32))
            ps_s3B = sub.enter_context(
                nc.psum_tensor(f"ps_s3B_{half}", [1, TB2], F32))

            r_vbs = Roll()
            r_sq2 = Roll()
            for fc in range(NFC):
                buf = fc % 2
                wv2 = [r_vbs.wait(buf)]
                if fc <= 1 and half == 0:
                    wv2.append(m_wo_done)
                m_vl2 = sch.dma("sync", "ds", vbs[buf][:, :],
                                vbf_h.ap()[fc, :, hc], waits=wv2)
                m_sq2 = sch.on(
                    "scalar",
                    lambda e, c, buf=buf: e.activation(
                        sq2b[buf][:, :], vbs[buf][:, :], AF.Square),
                    [m_vl2, r_sq2.wait(buf)], "aE")

                def mm_m2(e, c, fc=fc, buf=buf):
                    for ts2 in range(2):
                        e.matmul(ps_m2[:, ts2 * QB:(ts2 + 1) * QB],
                                 ast_sb[:, fc * 68:(fc + 1) * 68],
                                 vbs[buf][:, ts2 * QB:(ts2 + 1) * QB],
                                 start=(fc == 0), stop=False)
                    for ts2 in range(2):
                        e.matmul(ps_s3A[:, ts2 * QB:(ts2 + 1) * QB],
                                 ones_bf[:, 0:1],
                                 vbs[buf][:, ts2 * QB:(ts2 + 1) * QB],
                                 start=(fc == 0), stop=(fc == NFC - 1))
                    last = None
                    for ts2 in range(2):
                        last = e.matmul(ps_s3B[:, ts2 * QB:(ts2 + 1) * QB],
                                        ones_bf[:, 0:1],
                                        sq2b[buf][:, ts2 * QB:(ts2 + 1) * QB],
                                        start=(fc == 0), stop=(fc == NFC - 1))
                    return last
                m_m2 = sch.on("tensor", mm_m2, [m_vl2, m_sq2], "tE")
                r_vbs.used(buf, m_m2)
                r_sq2.used(buf, m_m2)

            def f23(e, c):
                e.tensor_scalar_mul(srow2["mean"][:, :], ps_s3A[:, :],
                                    1.0 / D)
                e.tensor_scalar_mul(srow2["tmp"][:, :], ps_s3B[:, :], 1.0 / D)
                e.tensor_mul(srow2["var"][:, :], srow2["mean"][:, :],
                             srow2["mean"][:, :])
                e.tensor_sub(srow2["var"][:, :], srow2["tmp"][:, :],
                             srow2["var"][:, :])
                e.tensor_scalar_add(srow2["var"][:, :], srow2["var"][:, :],
                                    EPS)
                e.tensor_scalar_mul(srow2["nmean"][:, :], srow2["mean"][:, :],
                                    -1.0)
                return e.tensor_copy(nmean2[:, :], srow2["nmean"][:, :])
            m_l3v = sch.on("vector", f23, [m_m2], "vE")

            def f23r(e, c):
                e.activation(srow2["tmp"][:, :], srow2["var"][:, :], AF.Ln)
                return e.activation(srow2["rstd"][:, :], srow2["tmp"][:, :],
                                    AF.Exp, scale=-0.5)
            m_l3r = sch.on("scalar", f23r, [m_l3v], "aE")

            def m2aug(e, c):
                for ts2 in range(2):
                    e.matmul(ps_m2[:, ts2 * QB:(ts2 + 1) * QB], csA_b[:, :],
                             nmean2[:, ts2 * QB:(ts2 + 1) * QB],
                             start=False, stop=True)
                last = None
                for ts2 in range(2):
                    last = e.matmul(ps_b2[0:68, ts2 * QB:(ts2 + 1) * QB],
                                    ones_f[:, 0:68],
                                    srow2["rstd"][:, ts2 * QB:(ts2 + 1) * QB],
                                    start=True, stop=True)
                return last
            m_m2a = sch.on("tensor", m2aug, [m_l3v, m_l3r], "tE")

            def m2ev(e, c):
                e.tensor_copy(r3bc[:, :], ps_b2[0:68, :])
                return e.tensor_mul(mid2[:, :], ps_m2[:, :], r3bc[:, :])
            m_m2e = sch.on("vector", m2ev, [m_m2a], "vE")

            for i2 in range(4):
                m_gr2 = sch.dma("gpsimd", "dg", grow2[f"r{i2}"][:, :],
                                mid2[64 + i2:65 + i2, :],
                                waits=[m_m2e] if i2 == 0 else ())
            m_e42 = sch.on("scalar",
                           lambda e, c: e.activation(e42[:, :],
                                                     mid2[64:68, :], AF.Exp),
                           [m_m2e], "aE")
            for i2 in range(4):
                m_ge2 = sch.dma("gpsimd", "dg", erow2[i2][:, :],
                                e42[i2:i2 + 1, :],
                                waits=[m_e42] if i2 == 0 else ())

            def f4gate2(e, c):
                g = grow2
                e.tensor_max(g["w0"][:, :], g["r0"][:, :], g["r1"][:, :])
                e.tensor_tensor(g["M"][:, :], g["r0"][:, :], g["r1"][:, :],
                                ALU.min)
                e.tensor_max(g["w1"][:, :], g["r2"][:, :], g["r3"][:, :])
                e.tensor_tensor(g["T2"][:, :], g["r2"][:, :], g["r3"][:, :],
                                ALU.min)
                e.tensor_max(g["den"][:, :], g["M"][:, :], g["T2"][:, :])
                e.tensor_tensor(g["T2"][:, :], g["w0"][:, :], g["w1"][:, :],
                                ALU.min)
                e.tensor_max(g["T2"][:, :], g["T2"][:, :], g["den"][:, :])
                for i2 in range(4):
                    e.tensor_tensor(g[f"r{i2}"][:, :], g[f"r{i2}"][:, :],
                                    g["T2"][:, :], ALU.is_ge)
                for i2 in range(4):
                    e.tensor_mul(erow2[i2][:, :], erow2[i2][:, :],
                                 g[f"r{i2}"][:, :])
                e.tensor_add(g["den"][:, :], erow2[0][:, :], erow2[1][:, :])
                e.tensor_add(g["den"][:, :], g["den"][:, :], erow2[2][:, :])
                e.tensor_add(g["den"][:, :], g["den"][:, :], erow2[3][:, :])
                e.reciprocal(g["rec"][:, :], g["den"][:, :])
                last = None
                for i2 in range(4):
                    last = e.tensor_mul(erow2[i2][:, :], erow2[i2][:, :],
                                        g["rec"][:, :])
                return last
            m_gate2 = sch.on("vector", f4gate2, [m_gr2, m_ge2], "vE")
            for i2 in range(4):
                m_g42 = sch.dma("gpsimd", "dg", g42[i2:i2 + 1, :],
                                erow2[i2][:, :],
                                waits=[m_gate2] if i2 == 0 else ())

            def f4g2(e, c):
                last = None
                for ts2 in range(2):
                    last = e.matmul(ps_b2[0:64, ts2 * QB:(ts2 + 1) * QB],
                                    etab_sb[:, :],
                                    g42[:, ts2 * QB:(ts2 + 1) * QB],
                                    start=True, stop=True)
                return last
            m_gmm2 = sch.on("tensor", f4g2, [m_g42, m_m2e], "tE")
            m_gmid2 = sch.on(
                "vector",
                lambda e, c: e.tensor_mul(gmid2[:, :], ps_b2[0:64, :],
                                          mid2[0:64, :]),
                [m_gmm2], "vE")

            sub.close()
            ps_d2 = [E2(nc.psum_tensor(f"ps_d2_{half}{i}", [128, QB], F32))
                     for i in range(2)]
            r_d2 = Roll()
            r_vrd = Roll()
            r_of = Roll()
            for oc in range(NFC):
                for ts2 in range(2):
                    buf = (oc * 2 + ts2) % 2
                    colg = half * TB2 + ts2 * QB

                    def mm_d2(e, c, oc=oc, ts2=ts2, buf=buf):
                        return e.matmul(ps_d2[buf][:, :],
                                        bpk_sb[:, oc * 128:(oc + 1) * 128],
                                        gmid2[:, ts2 * QB:(ts2 + 1) * QB],
                                        start=True, stop=True)
                    m_d2 = sch.on("tensor", mm_d2,
                                  [m_gmid2, r_d2.wait(buf)], "tE")
                    m_vrd = sch.dma("scalar", "da", vrd[buf][:, :],
                                    v_my.ap()[oc, :, colg:colg + QB],
                                    waits=[r_vrd.wait(buf)])

                    def oadd(e, c, buf=buf):
                        return e.tensor_add(of[buf][:, :], ps_d2[buf][:, :],
                                            vrd[buf][:, :])
                    m_oa = sch.on("vector", oadd,
                                  [m_d2, m_vrd, r_of.wait(buf)], "vE")
                    r_d2.used(buf, m_oa)
                    r_vrd.used(buf, m_oa)
                    m_ost = sch.dma("sync", "ds",
                                    out_my.ap()[oc, :, colg:colg + QB],
                                    of[buf][:, :], waits=[m_oa])
                    r_of.used(buf, m_ost)
                    st_out.append(m_ost)
            hs.close()

        sch.on("sync", lambda e, c: None, [("ds", sch.val("ds")),
                                           ("da", sch.val("da")),
                                           ("dg", sch.val("dg")),
                                           ("gE", sch.val("gE"))])
        _emit(nc, sch, semh, ectx)
    return nc



def _perm(role):
    idx = np.arange(T).reshape(8, 2, 512)
    if role == 1:
        idx = idx[:, ::-1, :]
    return idx.reshape(-1)


def _build_inputs(inputs, nonce):
    x = np.asarray(inputs["L0_attn_hidden_states"])
    Wq = np.asarray(inputs["L0_attn_Wq"])
    Wk = np.asarray(inputs["L0_attn_Wk"])
    Wv = np.asarray(inputs["L0_attn_Wv"])
    Wo = np.asarray(inputs["L0_attn_Wo"])
    A = np.asarray(inputs["L0_ffn_A"])
    B = np.asarray(inputs["L0_ffn_B"])
    Wr = np.asarray(inputs["L0_ffn_Wr"])

    invf = 1.0 / (10000.0 ** (np.arange(0, 64, 2, dtype=np.float64) / 64.0))
    freqs = invf[np.arange(64) // 2]

    wo_t = np.ascontiguousarray(
        Wo.reshape(NFC, 128, NFC, 128).transpose(2, 1, 0, 3)
        .reshape(NFC, 128, D)).astype(np.float32)
    a_stack = np.concatenate(
        [A.transpose(2, 0, 1).reshape(D, 64), Wr.T], axis=1)
    a_st = np.ascontiguousarray(
        a_stack.reshape(NFC, 128, 68)).astype(np.float32)
    b_pk = np.ascontiguousarray(
        B.transpose(0, 2, 1).reshape(64, D)).astype(np.float32)
    e_tab = np.zeros((4, 64), dtype=np.float32)
    for ee in range(4):
        e_tab[ee, ee * 16:(ee + 1) * 16] = 2.0
    pi_tab = np.zeros((64, 64), dtype=np.float32)
    for j in range(64):
        pi_tab[j ^ 1, j] = 1.0
    identf = np.eye(128, dtype=np.float32)

    w_sl = {}
    for par in range(2):
        sl = slice(par * DC, (par + 1) * DC)
        for nm, W in (("wq_t", Wq), ("wk_t", Wk), ("wv_t", Wv)):
            w_sl[(nm, par)] = np.ascontiguousarray(
                np.asarray(W)[:, sl].reshape(NFC, 128, NOC, 128)
                .transpose(2, 1, 0, 3).reshape(NOC, 128, D)
            ).astype(np.float32)

    in_maps = []
    for core in range(8):
        b = core // 4
        role = (core % 4) // 2
        par = core % 2
        perm = _perm(role)
        x_fm = np.ascontiguousarray(
            np.asarray(x)[b].T[:, perm].reshape(NFC, 128, T)
        ).astype(np.float32)
        ang_k = freqs[:, None] * perm[None, :].astype(np.float64)
        posq = (np.arange(NQB) * 1024 + role * 512)[:, None] \
            + np.arange(QB)[None, :]
        posq = posq.reshape(-1).astype(np.float64)
        ang_q = freqs[:, None] * posq[None, :]
        maskbk = np.zeros((8, 128, QB), dtype=np.float32)
        p_i = np.arange(128)[:, None]
        f_i = np.arange(QB)[None, :]
        for j in range(4):
            maskbk[j] = (f_i >= p_i + 128 * j).astype(np.float32)
        if role == 1:
            maskbk[4:8] = 1.0
        off = np.zeros((1, 16), dtype=np.int32)
        off[0, 0] = par * 16
        for bb in range(4):
            off[0, 1 + bb] = (par + 2 * bb)
        off[0, 5] = par
        off[0, 6] = 1 - par
        off[0, 7] = nonce
        in_maps.append({
            "x_fm": x_fm,
            "wq_t": w_sl[("wq_t", par)],
            "wk_t": w_sl[("wk_t", par)],
            "wv_t": w_sl[("wv_t", par)],
            "wo_t": wo_t,
            "a_st": a_st, "b_pk": b_pk, "e_tab": e_tab, "pi_tab": pi_tab,
            "rotck": np.cos(ang_k).astype(np.float32),
            "rotsk": np.sin(ang_k).astype(np.float32),
            "rotcq": np.cos(ang_q).astype(np.float32),
            "rotsq": np.sin(ang_q).astype(np.float32),
            "maskb": maskbk, "identf": identf, "offs": off,
        })
    return in_maps


def _assemble(results):
    out = np.zeros((2, T, D), dtype=np.float32)
    for core in range(8):
        b = core // 4
        role = (core % 4) // 2
        par = core % 2
        o = results[core]["out_my"].reshape(D, 4 * QB)
        for bb in range(4):
            g = par + 2 * bb
            t0 = 1024 * g + 512 * role
            out[b, t0:t0 + QB, :] = o[:, bb * QB:(bb + 1) * QB].T
    return out


_NONCE = [12345]
_NC_CACHE = {}


def kernel(**inputs):
    _NONCE[0] += 1
    in_maps = _build_inputs(inputs, _NONCE[0] * 7919 + 13)
    if 0 not in _NC_CACHE:
        _NC_CACHE[0] = _build(dbg=0)
    res = run_bass_kernel_spmd(_NC_CACHE[0], in_maps, core_ids=list(range(8)))
    return _assemble(res.results)
